# version 18
# baseline (speedup 1.0000x reference)
import contextlib
import os
import numpy as np
import concourse.bass as bass
import concourse.mybir as mybir
from concourse.bass_utils import run_bass_kernel_spmd

F32 = mybir.dt.float32
BF16 = mybir.dt.bfloat16
AF = mybir.ActivationFunctionType
ALU = mybir.AluOpType

D = 1024
INNER = 2048
NTOK = 4096
T = 512
NTILE = NTOK // T
TC = T // 128
SEQ_TILES = 2048 // T
DCN = D // 128
CCN = INNER // 128
NJ = 16
PAD = 4
EPS = 1e-5
N_CORES = 8
MMGAP = 256
P0GAP = 512
P0GAPM = 512
_SIM = os.environ.get("BASS_SIM_FP64") == "1"


def build_program():
    nc = bass.Bass("TRN2", target_bir_lowering=False,
                   detect_race_conditions=False,
                   dynamic_dma_scratch_size=4096)

    x_in = nc.dram_tensor("x", [NTOK, D], F32, kind="ExternalInput").ap()
    w_in_h = nc.dram_tensor("w_in", [D, 2 * INNER], F32, kind="ExternalInput").ap()
    conv_w_h = nc.dram_tensor("conv_w", [128, CCN, 4], F32, kind="ExternalInput").ap()
    conv_b_h = nc.dram_tensor("conv_b", [128, CCN], F32, kind="ExternalInput").ap()
    w_out_h = nc.dram_tensor("w_out", [INNER, D], F32, kind="ExternalInput").ap()
    wr_h = nc.dram_tensor("wr", [D, D], F32, kind="ExternalInput").ap()
    wi_h = nc.dram_tensor("wi", [D, D], F32, kind="ExternalInput").ap()
    ca_h = nc.dram_tensor("ca", [D, D], F32, kind="ExternalInput").ap()
    cb_h = nc.dram_tensor("cb", [D, D], F32, kind="ExternalInput").ap()
    sa_h = nc.dram_tensor("sa", [D, D], F32, kind="ExternalInput").ap()
    sb_h = nc.dram_tensor("sb", [D, D], F32, kind="ExternalInput").ap()
    cas_h = nc.dram_tensor("cas", [D, 128], F32, kind="ExternalInput").ap()
    cbs_h = nc.dram_tensor("cbs", [D, 128], F32, kind="ExternalInput").ap()
    sas_h = nc.dram_tensor("sas", [D, 128], F32, kind="ExternalInput").ap()
    sbs_h = nc.dram_tensor("sbs", [D, 128], F32, kind="ExternalInput").ap()
    bias_h = nc.dram_tensor("bias", [1, D], F32, kind="ExternalInput").ap()
    out_h = nc.dram_tensor("out", [NTOK, D], F32, kind="ExternalOutput").ap()
    xh_hbm = nc.dram_tensor("xh_scr", [NTOK, D], BF16).ap()
    xl_hbm = nc.dram_tensor("xl_scr", [NTOK, D], BF16).ap()
    mslice_hbm = nc.dram_tensor("m_slice", [256, D], BF16).ap()
    mfull_hbm = nc.dram_tensor("m_full", [8 * 256, D], BF16, addr_space="Shared").ap()

    x_t = x_in.rearrange("(n c p) d -> n p c d", c=TC, p=128)
    out_t = out_h.rearrange("(n c p) d -> n p c d", c=TC, p=128)
    xh_t = xh_hbm.rearrange("(n c p) d -> n p c d", c=TC, p=128)
    xl_t = xl_hbm.rearrange("(n c p) d -> n p c d", c=TC, p=128)
    w_in_c = w_in_h.rearrange("(k p) f -> k p f", p=128)
    w_out_c = w_out_h.rearrange("(k p) f -> k p f", p=128)
    wr_c = wr_h.rearrange("(k p) f -> k p f", p=128)
    wi_c = wi_h.rearrange("(k p) f -> k p f", p=128)
    ca_c = ca_h.rearrange("(k p) f -> k p f", p=128)
    cb_c = cb_h.rearrange("(k p) f -> k p f", p=128)
    sa_c = sa_h.rearrange("(k p) f -> k p f", p=128)
    sb_c = sb_h.rearrange("(k p) f -> k p f", p=128)
    cas_c = cas_h.rearrange("(k p) f -> k p f", p=128)
    cbs_c = cbs_h.rearrange("(k p) f -> k p f", p=128)
    sas_c = sas_h.rearrange("(k p) f -> k p f", p=128)
    sbs_c = sbs_h.rearrange("(k p) f -> k p f", p=128)
    mfull_h = mfull_hbm.rearrange("(c t p) f -> t p c f", c=8, t=2, p=128)

    es = contextlib.ExitStack()

    def sem(name):
        return es.enter_context(nc.semaphore(name))

    with es:
        ones_bf = es.enter_context(nc.sbuf_tensor("ones_bf", [1, 128], BF16))
        bias_bf = es.enter_context(nc.sbuf_tensor("bias_bf", [1, D], BF16))
        conv_w_sb = es.enter_context(nc.sbuf_tensor("conv_w_sb", [128, CCN, 4], F32))
        conv_b_sb = es.enter_context(nc.sbuf_tensor("conv_b_sb", [128, CCN], F32))

        sLd = sem("sLd"); sSum = sem("sSum"); sHi = sem("sHi"); sSplit = sem("sSplit")
        sP0 = sem("sP0"); sEvH = sem("sEvH"); sEvL = sem("sEvL"); sNeg = sem("sNeg")
        sStA = sem("sStA"); sMl = sem("sMl"); sMB = sem("sMB")
        sWL = sem("sWL")
        sX = sem("sX"); sXB = sem("sXB"); sXF = sem("sXF")
        sHg = sem("sHg"); sHa = sem("sHa"); sSG = sem("sSG"); sGLU = sem("sGLU")
        sCV = sem("sCV"); sSI = sem("sSI"); sWO = sem("sWO"); sZ1 = sem("sZ1")
        sSQ1 = sem("sSQ1"); sVE = sem("sVE"); sSD = sem("sSD")
        sX1 = sem("sX1"); sXHc = sem("sXHc"); sXL = sem("sXL"); sSt = sem("sSt")
        sBX = sem("sBX"); sBF = sem("sBF"); sFB = sem("sFB"); sZ2 = sem("sZ2")
        sSQ2 = sem("sSQ2"); sVE2 = sem("sVE2"); sSD2 = sem("sSD2")
        sOut = sem("sOut"); sOS = sem("sOS"); sVV = sem("sVV"); sTT = sem("sTT"); sAdd = sem("sAdd"); sCp = sem("sCp")

        es0 = contextlib.ExitStack()
        es_w = contextlib.ExitStack()
        es_st = contextlib.ExitStack()
        es_p0 = contextlib.ExitStack()

        def sb0(name, shape, dtype):
            return es0.enter_context(nc.sbuf_tensor(name, shape, dtype))

        Chi = [sb0(f"Chi{d}", [128, D], BF16) for d in range(DCN)]
        Clo = [sb0(f"Clo{d}", [128, D], BF16) for d in range(DCN)]
        Shi = [sb0(f"Shi{d}", [128, D], BF16) for d in range(DCN)]
        Slo = [sb0(f"Slo{d}", [128, D], BF16) for d in range(DCN)]
        Cslh = [sb0(f"Cslh{d}", [128, 128], BF16) for d in range(DCN)]
        Csll = [sb0(f"Csll{d}", [128, 128], BF16) for d in range(DCN)]
        Sslh = [sb0(f"Sslh{d}", [128, 128], BF16) for d in range(DCN)]
        Ssll = [sb0(f"Ssll{d}", [128, 128], BF16) for d in range(DCN)]
        negSh = [sb0(f"negSh{d}", [128, 128], BF16) for d in range(DCN)]
        negSl = [sb0(f"negSl{d}", [128, 128], BF16) for d in range(DCN)]
        AtUh = sb0("AtUh", [128, D], BF16)
        AtUl = sb0("AtUl", [128, D], BF16)
        BtUh = sb0("BtUh", [128, D], BF16)
        BtUl = sb0("BtUl", [128, D], BF16)
        AsTh = sb0("AsTh", [128, DCN, 128], BF16)
        AsTl = sb0("AsTl", [128, DCN, 128], BF16)
        BsTh = sb0("BsTh", [128, DCN, 128], BF16)
        BsTl = sb0("BsTl", [128, DCN, 128], BF16)
        mh_ev = sb0("mh_ev", [128, D], BF16)
        ml_ev = sb0("ml_ev", [128, D], BF16)
        wrh = [es_w.enter_context(nc.sbuf_tensor(f"wrh{d}", [128, D], BF16, side="right")) for d in range(DCN)]
        wrl = [es_w.enter_context(nc.sbuf_tensor(f"wrl{d}", [128, D], BF16, side="right")) for d in range(DCN)]
        wih = [es_w.enter_context(nc.sbuf_tensor(f"wih{d}", [128, D], BF16, side="right")) for d in range(DCN)]
        wil = [es_w.enter_context(nc.sbuf_tensor(f"wil{d}", [128, D], BF16, side="right")) for d in range(DCN)]
        stg = [es_st.enter_context(nc.sbuf_tensor(f"stg{s}", [128, 2, D], F32)) for s in range(3)]
        acc = [es_st.enter_context(nc.sbuf_tensor(f"pacc{s}", [128, 512], F32)) for s in range(4)]
        P = [es_p0.enter_context(nc.psum_tensor(f"P{s}", [128, 512], F32)) for s in range(8)]

        def atu_terms(d):
            return [(Cslh[d], wrh[d]), (Cslh[d], wrl[d]), (Csll[d], wrh[d]),
                    (Sslh[d], wih[d]), (Sslh[d], wil[d]), (Ssll[d], wih[d])]

        def btu_terms(d):
            return [(Cslh[d], wih[d]), (Cslh[d], wil[d]), (Csll[d], wih[d]),
                    (negSh[d], wrh[d]), (negSh[d], wrl[d]), (negSl[d], wrh[d])]

        def msl_terms(ic):
            return [(AsTh[:, ic, :], Chi[ic]), (AsTh[:, ic, :], Clo[ic]),
                    (AsTl[:, ic, :], Chi[ic]), (BsTh[:, ic, :], Shi[ic]),
                    (BsTh[:, ic, :], Slo[ic]), (BsTl[:, ic, :], Shi[ic])]

        NIT = 40
        _item_srcs = lambda d: [(ca_c[d], cb_c[d]), (sa_c[d], sb_c[d]),
                                (wr_c[d], wi_c[d]), (cas_c[d], cbs_c[d]),
                                (sas_c[d], sbs_c[d])]

        with nc.Block("p0") as blk:
            @blk.sync
            def _(sy):
                for k in range(NIT):
                    if k >= 3:
                        sy.wait_ge(sSplit, k - 2)
                    d, tp = k // 5, k % 5
                    s0, s1 = _item_srcs(d)[tp]
                    w = 128 if tp >= 3 else D
                    sy.dma_start(stg[k % 3][:, 0, 0:w], s0).then_inc(sLd, 16)
                    sy.dma_start(stg[k % 3][:, 1, 0:w], s1).then_inc(sLd, 16)
                sy.wait_ge(sEvL, 3)
                sy.dma_start(AsTh[:], AtUh[:], transpose=True).then_inc(sTT, 16)
                sy.dma_start(AsTl[:], AtUl[:], transpose=True).then_inc(sTT, 16)
                sy.wait_ge(sEvL, 4)
                sy.dma_start(BsTh[:], BtUh[:], transpose=True).then_inc(sTT, 16)
                sy.dma_start(BsTl[:], BtUl[:], transpose=True).then_inc(sTT, 16)
                sy.wait_ge(sEvL, 6)
                sy.dma_start(mslice_hbm[0:128, :], mh_ev[:]).then_inc(sStA, 16)
                sy.dma_start(mslice_hbm[128:256, :], ml_ev[:]).then_inc(sStA, 16)

            @blk.gpsimd
            def _(gp):
                for k in range(NIT):
                    d, tp = k // 5, k % 5
                    w = 128 if tp >= 3 else D
                    hi = [Chi, Shi, None, Cslh, Sslh][tp]
                    if tp == 2:
                        gp.wait_ge(sLd, 32 * (k + 1))
                        gp.tensor_copy(wrh[d][:], stg[k % 3][:, 0, :])
                        gp.tensor_copy(wih[d][:], stg[k % 3][:, 1, :]).then_inc(sHi, 1)
                    else:
                        gp.wait_ge(sSum, k + 1)
                        gp.tensor_copy(hi[d][:], stg[k % 3][:, 0, 0:w]).then_inc(sHi, 1)

            @blk.vector
            def _(v):
                v.memset(ones_bf[:], 1.0)
                for k in range(NIT):
                    d, tp = k // 5, k % 5
                    w = 128 if tp >= 3 else D
                    v.wait_ge(sLd, 32 * (k + 1))
                    if not _SIM:
                        v.nop(cycle_cnt=3072)
                    if tp == 2:
                        v.nop().then_inc(sSum, 1)
                        v.wait_ge(sHi, k + 1)
                        v.tensor_sub(wrl[d][:], stg[k % 3][:, 0, :], wrh[d][:])
                        v.tensor_sub(wil[d][:], stg[k % 3][:, 1, :], wih[d][:]).then_inc(sSplit, 1)
                    else:
                        v.tensor_add(stg[k % 3][:, 0, 0:w], stg[k % 3][:, 0, 0:w],
                                     stg[k % 3][:, 1, 0:w])
                        v.nop().then_inc(sSum, 1)
                        v.wait_ge(sHi, k + 1)
                        hi = [Chi, Shi, None, Cslh, Sslh][tp][d]
                        lo = [Clo, Slo, None, Csll, Ssll][tp][d]
                        if tp == 4:
                            v.tensor_sub(lo[:], stg[k % 3][:, 0, 0:w], hi[:])
                            v.tensor_scalar_mul(negSh[d][:], hi[:], -1.0)
                            v.scalar_tensor_tensor(
                                negSl[d][:], stg[k % 3][:, 0, 0:w], -1.0, hi[:],
                                ALU.mult, ALU.add).then_inc(sSplit, 1)
                        else:
                            v.tensor_sub(lo[:], stg[k % 3][:, 0, 0:w], hi[:]).then_inc(sSplit, 1)
                v.wait_ge(sP0, 1)
                for k in range(4):
                    v.tensor_copy(acc[k][:], P[k][:])
                    v.tensor_add(acc[k][:], acc[k][:], P[k + 4][:]).then_inc(sAdd, 1)
                for k in range(4):
                    hi = [AtUh, BtUh, AtUh, BtUh][k]
                    lo = [AtUl, BtUl, AtUl, BtUl][k]
                    h = k // 2
                    scl = (1.0 / D) if k % 2 == 0 else (-1.0 / D)
                    v.wait_ge(sEvH, k + 1)
                    v.scalar_tensor_tensor(
                        lo[:, 512 * h:512 * (h + 1)], acc[k][:], scl,
                        hi[:, 512 * h:512 * (h + 1)],
                        ALU.mult, ALU.subtract).then_inc(sEvL, 1)
                v.wait_ge(sP0, 2)
                v.tensor_copy(acc[0][:], P[0][:])
                v.tensor_add(acc[0][:], acc[0][:], P[1][:])
                v.tensor_add(acc[0][:], acc[0][:], P[4][:])
                v.tensor_add(acc[0][:], acc[0][:], P[5][:]).then_inc(sAdd, 1)
                v.tensor_copy(acc[2][:], P[2][:])
                v.tensor_add(acc[2][:], acc[2][:], P[3][:])
                v.tensor_add(acc[2][:], acc[2][:], P[6][:])
                v.tensor_add(acc[2][:], acc[2][:], P[7][:]).then_inc(sAdd, 1)
                for h in range(2):
                    v.wait_ge(sEvH, h + 5)
                    v.scalar_tensor_tensor(
                        ml_ev[:, 512 * h:512 * (h + 1)], acc[2 * h][:], 1.0,
                        mh_ev[:, 512 * h:512 * (h + 1)],
                        ALU.mult, ALU.subtract).then_inc(sEvL, 1)

            @blk.scalar
            def _(sc):
                for k in range(4):
                    hi = [AtUh, BtUh, AtUh, BtUh][k]
                    h = k // 2
                    sc.wait_ge(sAdd, k + 1)
                    sc.activation(hi[:, 512 * h:512 * (h + 1)], acc[k][:],
                                  AF.Copy,
                                  scale=(1.0 / D) if k % 2 == 0 else (-1.0 / D)
                                  ).then_inc(sEvH, 1)
                for h in range(2):
                    sc.wait_ge(sAdd, h + 5)
                    sc.activation(mh_ev[:, 512 * h:512 * (h + 1)], acc[2 * h][:],
                                  AF.Copy, scale=1.0).then_inc(sEvH, 1)

            @blk.tensor
            def _(t):
                def _mm(out, lt, rt, start, stop, gap=P0GAP):
                    t.matmul(out, lt, rt, start=start, stop=stop)
                    if not _SIM:
                        t.nop(cycle_cnt=gap)
                t.wait_ge(sSplit, NIT)
                for d in range(DCN):
                    q = d % 2
                    st = d < 2
                    sp = d >= DCN - 2
                    for ti in range(6):
                        la, ra = atu_terms(d)[ti]
                        lb, rb = btu_terms(d)[ti]
                        for h in range(2):
                            _mm(P[4 * q + 2 * h][:], la[:], ra[:, 512 * h:512 * (h + 1)],
                                start=(st and ti == 0), stop=(sp and ti == 5))
                            _mm(P[4 * q + 2 * h + 1][:], lb[:], rb[:, 512 * h:512 * (h + 1)],
                                start=(st and ti == 0), stop=(sp and ti == 5))
                t.nop().then_inc(sP0, 1)
                t.wait_ge(sTT, 64)
                for ti in range(6):
                    for ic in range(DCN):
                        lt, rt = msl_terms(ic)[ti]
                        st = (ti == 0 and ic < 4)
                        sp = (ti == 5 and ic >= 4)
                        _mm(P[[0, 1, 4, 5][ic % 4]][:], lt, rt[:, 0:512],
                            start=st, stop=sp, gap=P0GAPM)
                        _mm(P[[2, 3, 6, 7][ic % 4]][:], lt, rt[:, 512:1024],
                            start=st, stop=sp, gap=P0GAPM)
                t.nop().then_inc(sP0, 1)

        es_st.close()
        es_w.close()

        es_wb = contextlib.ExitStack()
        w_in_bf = [es_wb.enter_context(nc.sbuf_tensor(f"w_in_bf{d}", [128, 2 * INNER], BF16, side="right")) for d in range(DCN)]
        w_out_bf = [es_wb.enter_context(nc.sbuf_tensor(f"w_out_bf{c}", [128, D], BF16, side="right")) for c in range(CCN)]

        with nc.Block("p0b") as blk:
            @blk.gpsimd
            def _(gp):
                gp.wait_ge(sSplit, NIT)
                gp.dma_start(conv_w_sb[:], conv_w_h).then_inc(sWL, 16)
                gp.dma_start(conv_b_sb[:], conv_b_h).then_inc(sWL, 16)
                gp.dma_start(bias_bf[:], bias_h).then_inc(sWL, 16)
                for c in range(CCN):
                    gp.dma_start(w_out_bf[c][:], w_out_c[c]).then_inc(sWL, 16)
                gp.wait_ge(sP0, 1)
                for d in range(DCN):
                    gp.dma_start(w_in_bf[d][:], w_in_c[d]).then_inc(sWL, 16)
                gp.wait_ge(sStA, 32)
                gp.collective_compute(
                    "AllGather", ALU.bypass,
                    replica_groups=[list(range(N_CORES))],
                    ins=[mslice_hbm.opt()],
                    outs=[mfull_hbm.opt()],
                ).then_inc(sMB, 1)
        W_CONV = 16 * 2
        W_BIAS = 16 * 3
        W_WOUT = 16 * (3 + CCN)
        W_WIN = 16 * (3 + CCN + DCN)

        es0.close()
        es_p0.close()

        esA = contextlib.ExitStack()
        esA_p = contextlib.ExitStack()

        def sbA(name, shape, dtype):
            return esA.enter_context(nc.sbuf_tensor(name, shape, dtype))

        x_tm = [sbA(f"x_tm{s}", [128, TC, D], F32) for s in range(2)]
        x_bf = [sbA(f"x_bf{s}", [128, D], BF16) for s in range(2)]
        x_fm = [sbA(f"x_fm{s}", [128, DCN, T], BF16) for s in range(2)]
        sg = [sbA(f"sg{s}", [128, T], F32) for s in range(2)]
        glu = [sbA(f"glu{j}", [128, PAD + T], BF16) for j in range(NJ)]
        acc = [sbA(f"acc{s}", [128, T], F32) for s in range(2)]
        z1 = sbA("z1", [128, TC, D], F32)
        xh_tm = sbA("xh_tm", [128, TC, D], BF16)
        xl_tm = sbA("xl_tm", [128, TC, D], BF16)
        sqscr = sbA("sqscr", [128, D], BF16)
        sum1 = [sbA(f"sum1_{s}", [128, 4], F32) for s in range(2)]
        sumsq1 = [sbA(f"sumsq1_{s}", [128, 4], F32) for s in range(2)]
        stat1 = [sbA(f"stat1_{s}", [128, 16], F32) for s in range(2)]
        rb1 = [sbA(f"rb1_{s}", [128, 12], F32) for s in range(2)]
        pp = [esA_p.enter_context(nc.psum_tensor(f"pp{s}", [128, 1024], F32)) for s in range(4)]

        with nc.Block("pa") as blk:
            @blk.sync
            def _(sy):
                sy.wait_ge(sStA, 32)
                for i in range(NTILE):
                    if i >= 2:
                        sy.wait_ge(sZ1, 4 * (i - 1))
                    sy.dma_start(x_tm[i % 2][:], x_t[i]).then_inc(sX, 16)
                    for ts in range(TC):
                        tk = 4 * i + ts
                        sy.wait_ge(sXB, tk + 1)
                        if i >= 2 and ts == 0:
                            sy.wait_ge(sHa, NJ * (i - 1))
                        sy.dma_start(x_fm[i % 2][:, :, ts * 128:(ts + 1) * 128],
                                     x_bf[ts % 2][:], transpose=True).then_inc(sXF, 16)
                    if i >= 1:
                        sy.wait_ge(sXHc, 4 * i)
                        sy.dma_start(xh_t[i - 1], xh_tm[:]).then_inc(sSt, 16)
                        sy.wait_ge(sXL, 4 * i)
                        sy.dma_start(xl_t[i - 1], xl_tm[:]).then_inc(sSt, 16)
                sy.wait_ge(sXHc, 4 * NTILE)
                sy.dma_start(xh_t[NTILE - 1], xh_tm[:]).then_inc(sSt, 16)
                sy.wait_ge(sXL, 4 * NTILE)
                sy.dma_start(xl_t[NTILE - 1], xl_tm[:]).then_inc(sSt, 16)

            @blk.tensor
            def _(t):
                def _mm(*a, **k):
                    t.matmul(*a, **k)
                    if not _SIM:
                        t.nop(cycle_cnt=MMGAP)
                t.wait_ge(sWL, W_WIN)
                for i in range(NTILE):
                    for bh in range(NJ // 4):
                        js = [4 * bh + q for q in range(4)]
                        if bh == 0:
                            t.wait_ge(sXF, 16 * 4 * (i + 1))
                            if i >= 1:
                                t.wait_ge(sZ1, 4 * i)
                        for d in range(DCN):
                            for q in range(4):
                                j = js[q]
                                J = NJ * i + j
                                if d == 0 and J >= 4:
                                    t.wait_ge(sSG, J - 3)
                                _mm(pp[q][:, 0:512],
                                    w_in_bf[d][:, (NJ + j) * 128:(NJ + j + 1) * 128],
                                    x_fm[i % 2][:, d, :],
                                    start=(d == 0), stop=(d == DCN - 1))
                                if d == 0 and J >= 4:
                                    t.wait_ge(sGLU, J - 3)
                                _mm(pp[q][:, 512:1024],
                                    w_in_bf[d][:, j * 128:(j + 1) * 128],
                                    x_fm[i % 2][:, d, :],
                                    start=(d == 0), stop=(d == DCN - 1))
                        t.nop().then_inc(sHg, 4)
                        t.nop().then_inc(sHa, 4)
                    if i == 0:
                        t.wait_ge(sWL, W_WOUT)
                    t.wait_ge(sSI, NJ * (i + 1))
                    for c in range(CCN):
                        for ts in range(TC):
                            for db in range(2):
                                _mm(pp[ts][:, db * 512:(db + 1) * 512],
                                    glu[c][:, PAD + ts * 128:PAD + (ts + 1) * 128],
                                    w_out_bf[c][:, db * 512:(db + 1) * 512],
                                    start=(c == 0), stop=(c == CCN - 1))
                    t.nop().then_inc(sWO, 8)

            @blk.scalar
            def _(sc):
                for i in range(NTILE):
                    for ts in range(TC):
                        tk = 4 * i + ts
                        sc.wait_ge(sX, 16 * (i + 1))
                        if tk >= 2:
                            sc.wait_ge(sXF, 16 * (tk - 1))
                        sc.copy(x_bf[ts % 2][:], x_tm[i % 2][:, ts, :]).then_inc(sXB, 1)
                    if i >= 1:
                        for ts in range(TC):
                            z = 4 * (i - 1) + ts
                            sc.wait_ge(sX1, z + 1)
                            if i >= 2 and ts == 0:
                                sc.wait_ge(sSt, 32 * (i - 1))
                            sc.copy(xh_tm[:, ts, :], z1[:, ts, :]).then_inc(sXHc, 1)
                    for j in range(NJ):
                        J = NJ * i + j
                        sc.wait_ge(sHg, 4 * (4 * i + j // 4 + 1))
                        if J >= 2:
                            sc.wait_ge(sGLU, J - 1)
                        sc.activation(sg[j % 2][:], pp[j % 4][:, 0:512],
                                      AF.Sigmoid).then_inc(sSG, 1)
                        sc.wait_ge(sCV, J + 1)
                        sc.activation(glu[j][:, PAD:PAD + T], acc[j % 2][:],
                                      AF.Silu,
                                      bias=conv_b_sb[:, j:j + 1]).then_inc(sSI, 1)
                    for ts in range(TC):
                        z = 4 * i + ts
                        sc.wait_ge(sZ1, z + 1)
                        sc.activation(sqscr[:], z1[:, ts, :], AF.Square,
                                      accum_out=sumsq1[i % 2][:, ts:ts + 1]).then_inc(sSQ1, 1)
                    sc.wait_ge(sVE, i + 1)
                    sc.activation(rb1[i % 2][:, 8:12], stat1[i % 2][:, 8:12],
                                  AF.Sqrt).then_inc(sSD, 1)
                for ts in range(TC):
                    z = 4 * (NTILE - 1) + ts
                    sc.wait_ge(sX1, z + 1)
                    if ts == 0:
                        sc.wait_ge(sSt, 32 * (NTILE - 1))
                    sc.copy(xh_tm[:, ts, :], z1[:, ts, :]).then_inc(sXHc, 1)

            @blk.vector
            def _(v):
                vv_c = [0]
                v.wait_ge(sStA, 32)
                for i in range(NTILE):
                    if i % SEQ_TILES == 0:
                        if i > 0:
                            v.wait_ge(sWO, 8 * i)
                        for j in range(NJ):
                            v.memset(glu[j][:, 1:PAD], 0.0)
                    for j in range(NJ):
                        J = NJ * i + j
                        v.wait_ge(sSG, J + 1)
                        v.wait_ge(sHa, 4 * (4 * i + j // 4 + 1))
                        if j == 0 and i > 0 and i % SEQ_TILES != 0:
                            v.wait_ge(sWO, 8 * i)
                        if j == 0 and i == 0:
                            v.wait_ge(sWL, W_CONV)
                        v.tensor_mul(glu[j][:, PAD:PAD + T],
                                     pp[j % 4][:, 512:1024], sg[j % 2][:]).then_inc(sGLU, 1)
                        if J >= 2:
                            v.wait_ge(sSI, J - 1)
                        v.tensor_scalar_mul(acc[j % 2][:], glu[j][:, 1:1 + T],
                                            conv_w_sb[:, j, 0:1])
                        for k in range(1, 4):
                            v.scalar_tensor_tensor(acc[j % 2][:], glu[j][:, 1 + k:1 + k + T],
                                                   conv_w_sb[:, j, k:k + 1],
                                                   acc[j % 2][:], ALU.mult, ALU.add)
                        v.tensor_copy(glu[j][:, 1:PAD], glu[j][:, 1 + T:PAD + T]).then_inc(sCV, 1)
                    for ts in range(TC):
                        z = 4 * i + ts
                        v.wait_ge(sWO, 8 * (i + 1))
                        if ts == 0:
                            v.wait_ge(sX, 16 * (i + 1))
                        if z >= 4:
                            v.wait_ge(sSQ1, z - 3)
                            v.wait_ge(sXHc, z - 3)
                        v.scalar_tensor_tensor(z1[:, ts, :], pp[ts][:], 1.0,
                                               x_tm[i % 2][:, ts, :], ALU.mult, ALU.add,
                                               accum_out=sum1[i % 2][:, ts:ts + 1]).then_inc(sZ1, 1)
                    v.wait_ge(sSQ1, 4 * (i + 1))
                    v.wait_ge(sZ1, 4 * (i + 1))
                    st = stat1[i % 2]
                    v.tensor_scalar_mul(st[:, 0:4], sum1[i % 2][:], -1.0 / D)
                    v.tensor_scalar(st[:, 4:8], sumsq1[i % 2][:], 1.0 / D, EPS,
                                    ALU.mult, ALU.add)
                    v.tensor_scalar_mul(st[:, 8:12], sum1[i % 2][:],
                                        1.0 / D).then_inc(sVV, 1)
                    vv_c[0] += 1
                    v.wait_ge(sVV, vv_c[0])
                    v.tensor_mul(st[:, 12:16], st[:, 8:12], st[:, 0:4]).then_inc(sVV, 1)
                    vv_c[0] += 1
                    v.wait_ge(sVV, vv_c[0])
                    v.tensor_add(st[:, 8:12], st[:, 4:8], st[:, 12:16]).then_inc(sVE, 1)
                    v.wait_ge(sSD, i + 1)
                    v.reciprocal(rb1[i % 2][:, 0:4], rb1[i % 2][:, 8:12]).then_inc(sVV, 1)
                    vv_c[0] += 1
                    v.wait_ge(sVV, vv_c[0])
                    v.tensor_mul(rb1[i % 2][:, 4:8], st[:, 0:4],
                                 rb1[i % 2][:, 0:4]).then_inc(sVV, 1)
                    vv_c[0] += 1
                    v.wait_ge(sVV, vv_c[0])
                    for ts in range(TC):
                        z = 4 * i + ts
                        v.tensor_scalar_mul(z1[:, ts, :], z1[:, ts, :],
                                            rb1[i % 2][:, ts:ts + 1])
                        v.tensor_scalar_add(z1[:, ts, :], z1[:, ts, :],
                                            rb1[i % 2][:, 4 + ts:5 + ts]).then_inc(sX1, 1)
                    for ts in range(TC):
                        z = 4 * i + ts
                        v.wait_ge(sXHc, z + 1)
                        if ts == 0 and i >= 1:
                            v.wait_ge(sSt, 32 * i)
                        v.tensor_sub(xl_tm[:, ts, :], z1[:, ts, :],
                                     xh_tm[:, ts, :]).then_inc(sXL, 1)

        esA.close()
        esA_p.close()
        es_wb.close()

        esB = contextlib.ExitStack()
        esB_p = contextlib.ExitStack()

        def sbB(name, shape, dtype):
            return esB.enter_context(nc.sbuf_tensor(name, shape, dtype))

        Mh2 = sbB("Mh2", [128, DCN, D], BF16)
        Ml2 = sbB("Ml2", [128, DCN, D], BF16)
        xh2 = [sbB(f"xh2_{s}", [128, TC, D], BF16) for s in range(2)]
        xl2 = [sbB(f"xl2_{s}", [128, TC, D], BF16) for s in range(2)]
        xfh = [sbB(f"xfh{s}", [128, DCN, T], BF16) for s in range(2)]
        xfl = [sbB(f"xfl{s}", [128, DCN, T], BF16) for s in range(2)]
        z2 = sbB("z2", [128, TC, D], F32)
        outb = [sbB(f"outb{s}", [128, TC, D], F32) for s in range(2)]
        sqscr2 = sbB("sqscr2", [128, D], BF16)
        sum2 = [sbB(f"sum2_{s}", [128, 4], F32) for s in range(2)]
        sumsq2 = [sbB(f"sumsq2_{s}", [128, 4], F32) for s in range(2)]
        stat2 = [sbB(f"stat2_{s}", [128, 16], F32) for s in range(2)]
        rb2 = [sbB(f"rb2_{s}", [128, 12], F32) for s in range(2)]
        pf = [esB_p.enter_context(nc.psum_tensor(f"pf{s}", [128, D], F32)) for s in range(4)]

        with nc.Block("pb") as blk:
            @blk.sync
            def _(sy):
                sy.wait_ge(sSt, 32 * NTILE)
                sy.wait_ge(sMB, 1)
                sy.dma_start(Mh2[:], mfull_h[0]).then_inc(sMB, 16)
                sy.dma_start(Ml2[:], mfull_h[1]).then_inc(sMB, 16)
                for i in range(NTILE):
                    if i >= 2:
                        sy.wait_ge(sZ2, 4 * (i - 1))
                    sy.dma_start(xh2[i % 2][:], xh_t[i]).then_inc(sBX, 16)
                    sy.dma_start(xl2[i % 2][:], xl_t[i]).then_inc(sBX, 16)
                    if i >= 1:
                        sy.wait_ge(sOut, 4 * i)
                        sy.dma_start(out_t[i - 1], outb[(i - 1) % 2][:]).then_inc(sOS, 16)
                sy.wait_ge(sOut, 4 * NTILE)
                sy.dma_start(out_t[NTILE - 1], outb[(NTILE - 1) % 2][:]).then_inc(sOS, 16)
                sy.wait_ge(sOS, 16 * NTILE)

            @blk.scalar
            def _(sc):
                for i in range(NTILE):
                    for ts in range(TC):
                        sc.wait_ge(sBX, 32 * (i + 1))
                        if i >= 2 and ts == 0:
                            sc.wait_ge(sFB, 8 * (i - 1))
                        sc.dma_start(xfh[i % 2][:, :, ts * 128:(ts + 1) * 128],
                                     xh2[i % 2][:, ts, :], transpose=True).then_inc(sBF, 16)
                        sc.dma_start(xfl[i % 2][:, :, ts * 128:(ts + 1) * 128],
                                     xl2[i % 2][:, ts, :], transpose=True).then_inc(sBF, 16)
                    for ts in range(TC):
                        z = 4 * i + ts
                        sc.wait_ge(sZ2, z + 1)
                        sc.activation(sqscr2[:], z2[:, ts, :], AF.Square,
                                      accum_out=sumsq2[i % 2][:, ts:ts + 1]).then_inc(sSQ2, 1)
                    sc.wait_ge(sVE2, i + 1)
                    sc.activation(rb2[i % 2][:, 8:12], stat2[i % 2][:, 8:12],
                                  AF.Sqrt).then_inc(sSD2, 1)

            @blk.tensor
            def _(t):
                def _mm(*a, **k):
                    t.matmul(*a, **k)
                    if not _SIM:
                        t.nop(cycle_cnt=MMGAP)
                t.wait_ge(sWL, W_BIAS)
                t.wait_ge(sMB, 33)
                for i in range(NTILE):
                    t.wait_ge(sBF, 16 * 8 * (i + 1))
                    if i >= 1:
                        t.wait_ge(sZ2, 4 * i)
                    for step in range(25):
                        for ts in range(TC):
                            for ob in range(2):
                                psl = pf[ts][:, ob * 512:(ob + 1) * 512]
                                if step == 0:
                                    _mm(psl, ones_bf[:],
                                        bias_bf[:, ob * 512:(ob + 1) * 512],
                                        start=True, stop=False)
                                else:
                                    s = step - 1
                                    ic, which = s // 3, s % 3
                                    lx = xfh[i % 2][:, ic, ts * 128:(ts + 1) * 128]
                                    ll = xfl[i % 2][:, ic, ts * 128:(ts + 1) * 128]
                                    if which == 0:
                                        _mm(psl, lx, Mh2[:, ic, ob * 512:(ob + 1) * 512],
                                            start=False, stop=False)
                                    elif which == 1:
                                        _mm(psl, lx, Ml2[:, ic, ob * 512:(ob + 1) * 512],
                                            start=False, stop=False)
                                    else:
                                        _mm(psl, ll, Mh2[:, ic, ob * 512:(ob + 1) * 512],
                                            start=False, stop=(step == 24))
                    t.nop().then_inc(sFB, 8)

            @blk.vector
            def _(v):
                vv_c = [32]
                for i in range(NTILE):
                    for ts in range(TC):
                        z = 4 * i + ts
                        v.wait_ge(sFB, 8 * (i + 1))
                        if z >= 4:
                            v.wait_ge(sSQ2, z - 3)
                        v.tensor_add(z2[:, ts, :], pf[ts][:], xh2[i % 2][:, ts, :])
                        v.scalar_tensor_tensor(z2[:, ts, :], xl2[i % 2][:, ts, :], 1.0,
                                               z2[:, ts, :], ALU.mult, ALU.add,
                                               accum_out=sum2[i % 2][:, ts:ts + 1]).then_inc(sZ2, 1)
                    v.wait_ge(sSQ2, 4 * (i + 1))
                    v.wait_ge(sZ2, 4 * (i + 1))
                    st = stat2[i % 2]
                    v.tensor_scalar_mul(st[:, 0:4], sum2[i % 2][:], -1.0 / D)
                    v.tensor_scalar(st[:, 4:8], sumsq2[i % 2][:], 1.0 / D, EPS,
                                    ALU.mult, ALU.add)
                    v.tensor_scalar_mul(st[:, 8:12], sum2[i % 2][:],
                                        1.0 / D).then_inc(sVV, 1)
                    vv_c[0] += 1
                    v.wait_ge(sVV, vv_c[0])
                    v.tensor_mul(st[:, 12:16], st[:, 8:12], st[:, 0:4]).then_inc(sVV, 1)
                    vv_c[0] += 1
                    v.wait_ge(sVV, vv_c[0])
                    v.tensor_add(st[:, 8:12], st[:, 4:8], st[:, 12:16]).then_inc(sVE2, 1)
                    v.wait_ge(sSD2, i + 1)
                    v.reciprocal(rb2[i % 2][:, 0:4], rb2[i % 2][:, 8:12]).then_inc(sVV, 1)
                    vv_c[0] += 1
                    v.wait_ge(sVV, vv_c[0])
                    v.tensor_mul(rb2[i % 2][:, 4:8], st[:, 0:4],
                                 rb2[i % 2][:, 0:4]).then_inc(sVV, 1)
                    vv_c[0] += 1
                    v.wait_ge(sVV, vv_c[0])
                    for ts in range(TC):
                        z = 4 * i + ts
                        if i >= 2 and ts == 0:
                            v.wait_ge(sOS, 16 * (i - 1))
                        v.tensor_scalar_mul(outb[i % 2][:, ts, :], z2[:, ts, :],
                                            rb2[i % 2][:, ts:ts + 1])
                        v.tensor_scalar_add(outb[i % 2][:, ts, :], outb[i % 2][:, ts, :],
                                            rb2[i % 2][:, 4 + ts:5 + ts]).then_inc(sOut, 1)

            @blk.gpsimd
            def _(gp):
                gp.engine_nop()

        esB.close()
        esB_p.close()
    return nc


def _host_prepare(inputs):
    x = np.ascontiguousarray(np.asarray(inputs["L0_attn_x"], dtype=np.float32))
    w_in = np.ascontiguousarray(np.asarray(inputs["L0_attn_W_in"], dtype=np.float32))
    conv_w = np.asarray(inputs["L0_attn_conv_w"], dtype=np.float32)
    conv_b = np.asarray(inputs["L0_attn_conv_b"], dtype=np.float32)
    w_out = np.ascontiguousarray(np.asarray(inputs["L0_attn_W_out"], dtype=np.float32))
    wr = np.ascontiguousarray(np.asarray(inputs["L0_ffn_w_freq_real"], dtype=np.float32))
    wi = np.ascontiguousarray(np.asarray(inputs["L0_ffn_w_freq_imag"], dtype=np.float32))
    bias = np.ascontiguousarray(np.asarray(inputs["L0_ffn_bias"], dtype=np.float32).reshape(1, D))

    conv_w_r = np.ascontiguousarray(conv_w.reshape(CCN, 128, 4).transpose(1, 0, 2))
    conv_b_r = np.ascontiguousarray(conv_b.reshape(CCN, 128).T)

    n = np.arange(D, dtype=np.float64)
    ang = (2.0 * np.pi / D) * np.outer(n, n)
    c64 = np.cos(ang)
    s64 = np.sin(ang)
    ca = c64.astype(np.float32)
    cb = (c64 - ca.astype(np.float64)).astype(np.float32)
    sa = s64.astype(np.float32)
    sb_ = (s64 - sa.astype(np.float64)).astype(np.float32)

    common = {
        "w_in": w_in, "conv_w": conv_w_r, "conv_b": conv_b_r, "w_out": w_out,
        "wr": wr, "wi": wi, "ca": np.ascontiguousarray(ca),
        "cb": np.ascontiguousarray(cb), "sa": np.ascontiguousarray(sa),
        "sb": np.ascontiguousarray(sb_), "bias": bias,
    }
    B = x.shape[0]
    per = B // N_CORES
    in_maps = []
    for c in range(N_CORES):
        m = dict(common)
        m["x"] = np.ascontiguousarray(x[c * per:(c + 1) * per].reshape(NTOK, D))
        sl = slice(128 * c, 128 * (c + 1))
        m["cas"] = np.ascontiguousarray(ca[:, sl])
        m["cbs"] = np.ascontiguousarray(cb[:, sl])
        m["sas"] = np.ascontiguousarray(sa[:, sl])
        m["sbs"] = np.ascontiguousarray(sb_[:, sl])
        in_maps.append(m)
    return in_maps, x.shape


def kernel(L0_attn_x, L0_attn_W_in, L0_attn_conv_w, L0_attn_conv_b,
           L0_attn_W_out, L0_ffn_w_freq_real, L0_ffn_w_freq_imag, L0_ffn_bias):
    inputs = {
        "L0_attn_x": L0_attn_x, "L0_attn_W_in": L0_attn_W_in,
        "L0_attn_conv_w": L0_attn_conv_w, "L0_attn_conv_b": L0_attn_conv_b,
        "L0_attn_W_out": L0_attn_W_out, "L0_ffn_w_freq_real": L0_ffn_w_freq_real,
        "L0_ffn_w_freq_imag": L0_ffn_w_freq_imag, "L0_ffn_bias": L0_ffn_bias,
    }
    in_maps, xshape = _host_prepare(inputs)
    nc = build_program()
    res = run_bass_kernel_spmd(nc, in_maps, core_ids=list(range(N_CORES)))
    per = xshape[0] // N_CORES
    out = np.empty(xshape, dtype=np.float32)
    for c in range(N_CORES):
        out[c * per:(c + 1) * per] = np.asarray(
            res.results[c]["out"]).reshape(per, xshape[1], xshape[2])
    return out


# version 19
# speedup vs baseline: 1.0148x; 1.0148x over previous
import contextlib
import os
import numpy as np
import concourse.bass as bass
import concourse.mybir as mybir
from concourse.bass_utils import run_bass_kernel_spmd

F32 = mybir.dt.float32
BF16 = mybir.dt.bfloat16
AF = mybir.ActivationFunctionType
ALU = mybir.AluOpType

D = 1024
INNER = 2048
NTOK = 4096
T = 512
NTILE = NTOK // T
TC = T // 128
SEQ_TILES = 2048 // T
DCN = D // 128
CCN = INNER // 128
NJ = 16
PAD = 4
EPS = 1e-5
N_CORES = 8
MMGAP = 256
P0GAP = 512
P0GAPM = 512
_SIM = os.environ.get("BASS_SIM_FP64") == "1"


def build_program():
    nc = bass.Bass("TRN2", target_bir_lowering=False,
                   detect_race_conditions=False,
                   dynamic_dma_scratch_size=4096)

    x_in = nc.dram_tensor("x", [NTOK, D], F32, kind="ExternalInput").ap()
    w_in_h = nc.dram_tensor("w_in", [D, 2 * INNER], F32, kind="ExternalInput").ap()
    conv_w_h = nc.dram_tensor("conv_w", [128, CCN, 4], F32, kind="ExternalInput").ap()
    conv_b_h = nc.dram_tensor("conv_b", [128, CCN], F32, kind="ExternalInput").ap()
    w_out_h = nc.dram_tensor("w_out", [INNER, D], F32, kind="ExternalInput").ap()
    wr_h = nc.dram_tensor("wr", [D, D], F32, kind="ExternalInput").ap()
    wi_h = nc.dram_tensor("wi", [D, D], F32, kind="ExternalInput").ap()
    ca_h = nc.dram_tensor("ca", [D, D], F32, kind="ExternalInput").ap()
    cb_h = nc.dram_tensor("cb", [D, D], F32, kind="ExternalInput").ap()
    sa_h = nc.dram_tensor("sa", [D, D], F32, kind="ExternalInput").ap()
    sb_h = nc.dram_tensor("sb", [D, D], F32, kind="ExternalInput").ap()
    cas_h = nc.dram_tensor("cas", [D, 128], F32, kind="ExternalInput").ap()
    cbs_h = nc.dram_tensor("cbs", [D, 128], F32, kind="ExternalInput").ap()
    sas_h = nc.dram_tensor("sas", [D, 128], F32, kind="ExternalInput").ap()
    sbs_h = nc.dram_tensor("sbs", [D, 128], F32, kind="ExternalInput").ap()
    bias_h = nc.dram_tensor("bias", [1, D], F32, kind="ExternalInput").ap()
    out_h = nc.dram_tensor("out", [NTOK, D], F32, kind="ExternalOutput").ap()
    xh_hbm = nc.dram_tensor("xh_scr", [NTOK, D], BF16).ap()
    xl_hbm = nc.dram_tensor("xl_scr", [NTOK, D], BF16).ap()
    mslice_hbm = nc.dram_tensor("m_slice", [256, D], BF16).ap()
    mfull_hbm = nc.dram_tensor("m_full", [8 * 256, D], BF16, addr_space="Shared").ap()

    x_t = x_in.rearrange("(n c p) d -> n p c d", c=TC, p=128)
    out_t = out_h.rearrange("(n c p) d -> n p c d", c=TC, p=128)
    xh_t = xh_hbm.rearrange("(n c p) d -> n p c d", c=TC, p=128)
    xl_t = xl_hbm.rearrange("(n c p) d -> n p c d", c=TC, p=128)
    w_in_c = w_in_h.rearrange("(k p) f -> k p f", p=128)
    w_out_c = w_out_h.rearrange("(k p) f -> k p f", p=128)
    wr_c = wr_h.rearrange("(k p) f -> k p f", p=128)
    wi_c = wi_h.rearrange("(k p) f -> k p f", p=128)
    ca_c = ca_h.rearrange("(k p) f -> k p f", p=128)
    cb_c = cb_h.rearrange("(k p) f -> k p f", p=128)
    sa_c = sa_h.rearrange("(k p) f -> k p f", p=128)
    sb_c = sb_h.rearrange("(k p) f -> k p f", p=128)
    cas_c = cas_h.rearrange("(k p) f -> k p f", p=128)
    cbs_c = cbs_h.rearrange("(k p) f -> k p f", p=128)
    sas_c = sas_h.rearrange("(k p) f -> k p f", p=128)
    sbs_c = sbs_h.rearrange("(k p) f -> k p f", p=128)
    mfull_h = mfull_hbm.rearrange("(c t p) f -> t p c f", c=8, t=2, p=128)

    es = contextlib.ExitStack()

    def sem(name):
        return es.enter_context(nc.semaphore(name))

    with es:
        ones_bf = es.enter_context(nc.sbuf_tensor("ones_bf", [1, 128], BF16))
        bias_bf = es.enter_context(nc.sbuf_tensor("bias_bf", [1, D], BF16))
        conv_w_sb = es.enter_context(nc.sbuf_tensor("conv_w_sb", [128, CCN, 4], F32))
        conv_b_sb = es.enter_context(nc.sbuf_tensor("conv_b_sb", [128, CCN], F32))

        sLd = sem("sLd"); sSum = sem("sSum"); sHi = sem("sHi"); sSplit = sem("sSplit")
        sP0 = sem("sP0"); sEvH = sem("sEvH"); sEvL = sem("sEvL"); sNeg = sem("sNeg")
        sStA = sem("sStA"); sMl = sem("sMl"); sMB = sem("sMB")
        sWL = sem("sWL")
        sX = sem("sX"); sXB = sem("sXB"); sXF = sem("sXF")
        sHg = sem("sHg"); sHa = sem("sHa"); sSG = sem("sSG"); sGLU = sem("sGLU")
        sCV = sem("sCV"); sSI = sem("sSI"); sWO = sem("sWO"); sZ1 = sem("sZ1")
        sSQ1 = sem("sSQ1"); sVE = sem("sVE"); sSD = sem("sSD")
        sX1 = sem("sX1"); sXHc = sem("sXHc"); sXL = sem("sXL"); sSt = sem("sSt")
        sBX = sem("sBX"); sBF = sem("sBF"); sFB = sem("sFB"); sZ2 = sem("sZ2")
        sSQ2 = sem("sSQ2"); sVE2 = sem("sVE2"); sSD2 = sem("sSD2")
        sOut = sem("sOut"); sOS = sem("sOS"); sVV = sem("sVV"); sTT = sem("sTT"); sAdd = sem("sAdd"); sCp = sem("sCp")

        es0 = contextlib.ExitStack()
        es_w = contextlib.ExitStack()
        es_st = contextlib.ExitStack()
        es_p0 = contextlib.ExitStack()

        def sb0(name, shape, dtype):
            return es0.enter_context(nc.sbuf_tensor(name, shape, dtype))

        Chi = [sb0(f"Chi{d}", [128, D], BF16) for d in range(DCN)]
        Clo = [sb0(f"Clo{d}", [128, D], BF16) for d in range(DCN)]
        Shi = [sb0(f"Shi{d}", [128, D], BF16) for d in range(DCN)]
        Slo = [sb0(f"Slo{d}", [128, D], BF16) for d in range(DCN)]
        Cslh = [sb0(f"Cslh{d}", [128, 128], BF16) for d in range(DCN)]
        Csll = [sb0(f"Csll{d}", [128, 128], BF16) for d in range(DCN)]
        Sslh = [sb0(f"Sslh{d}", [128, 128], BF16) for d in range(DCN)]
        Ssll = [sb0(f"Ssll{d}", [128, 128], BF16) for d in range(DCN)]
        negSh = [sb0(f"negSh{d}", [128, 128], BF16) for d in range(DCN)]
        negSl = [sb0(f"negSl{d}", [128, 128], BF16) for d in range(DCN)]
        AtUh = sb0("AtUh", [128, D], BF16)
        AtUl = sb0("AtUl", [128, D], BF16)
        BtUh = sb0("BtUh", [128, D], BF16)
        BtUl = sb0("BtUl", [128, D], BF16)
        AsTh = sb0("AsTh", [128, DCN, 128], BF16)
        AsTl = sb0("AsTl", [128, DCN, 128], BF16)
        BsTh = sb0("BsTh", [128, DCN, 128], BF16)
        BsTl = sb0("BsTl", [128, DCN, 128], BF16)
        mh_ev = sb0("mh_ev", [128, D], BF16)
        ml_ev = sb0("ml_ev", [128, D], BF16)
        wrh = [es_w.enter_context(nc.sbuf_tensor(f"wrh{d}", [128, D], BF16, side="right")) for d in range(DCN)]
        wrl = [es_w.enter_context(nc.sbuf_tensor(f"wrl{d}", [128, D], BF16, side="right")) for d in range(DCN)]
        wih = [es_w.enter_context(nc.sbuf_tensor(f"wih{d}", [128, D], BF16, side="right")) for d in range(DCN)]
        wil = [es_w.enter_context(nc.sbuf_tensor(f"wil{d}", [128, D], BF16, side="right")) for d in range(DCN)]
        stg = [es_st.enter_context(nc.sbuf_tensor(f"stg{s}", [128, 2, D], F32)) for s in range(3)]
        acc = [es_st.enter_context(nc.sbuf_tensor(f"pacc{s}", [128, 512], F32)) for s in range(4)]
        P = [es_p0.enter_context(nc.psum_tensor(f"P{s}", [128, 512], F32)) for s in range(8)]

        def atu_terms(d):
            return [(Cslh[d], wrh[d]), (Cslh[d], wrl[d]), (Csll[d], wrh[d]),
                    (Sslh[d], wih[d]), (Sslh[d], wil[d]), (Ssll[d], wih[d])]

        def btu_terms(d):
            return [(Cslh[d], wih[d]), (Cslh[d], wil[d]), (Csll[d], wih[d]),
                    (negSh[d], wrh[d]), (negSh[d], wrl[d]), (negSl[d], wrh[d])]

        def msl_terms(ic):
            return [(AsTh[:, ic, :], Chi[ic]), (AsTh[:, ic, :], Clo[ic]),
                    (AsTl[:, ic, :], Chi[ic]), (BsTh[:, ic, :], Shi[ic]),
                    (BsTh[:, ic, :], Slo[ic]), (BsTl[:, ic, :], Shi[ic])]

        NIT = 40
        _item_srcs = lambda d: [(ca_c[d], cb_c[d]), (sa_c[d], sb_c[d]),
                                (wr_c[d], wi_c[d]), (cas_c[d], cbs_c[d]),
                                (sas_c[d], sbs_c[d])]

        with nc.Block("p0") as blk:
            @blk.sync
            def _(sy):
                for k in range(NIT):
                    if k >= 3:
                        sy.wait_ge(sSplit, k - 2)
                    d, tp = k // 5, k % 5
                    s0, s1 = _item_srcs(d)[tp]
                    w = 128 if tp >= 3 else D
                    sy.dma_start(stg[k % 3][:, 0, 0:w], s0).then_inc(sLd, 16)
                    sy.dma_start(stg[k % 3][:, 1, 0:w], s1).then_inc(sLd, 16)
                sy.wait_ge(sEvL, 3)
                sy.dma_start(AsTh[:], AtUh[:], transpose=True).then_inc(sTT, 16)
                sy.dma_start(AsTl[:], AtUl[:], transpose=True).then_inc(sTT, 16)
                sy.wait_ge(sEvL, 4)
                sy.dma_start(BsTh[:], BtUh[:], transpose=True).then_inc(sTT, 16)
                sy.dma_start(BsTl[:], BtUl[:], transpose=True).then_inc(sTT, 16)
                sy.wait_ge(sEvL, 6)
                sy.dma_start(mslice_hbm[0:128, :], mh_ev[:]).then_inc(sStA, 16)
                sy.dma_start(mslice_hbm[128:256, :], ml_ev[:]).then_inc(sStA, 16)

            @blk.gpsimd
            def _(gp):
                for k in range(NIT):
                    d, tp = k // 5, k % 5
                    w = 128 if tp >= 3 else D
                    hi = [Chi, Shi, None, Cslh, Sslh][tp]
                    if tp == 2:
                        gp.wait_ge(sLd, 32 * (k + 1))
                        gp.tensor_copy(wrh[d][:], stg[k % 3][:, 0, :])
                        gp.tensor_copy(wih[d][:], stg[k % 3][:, 1, :]).then_inc(sHi, 1)
                    else:
                        gp.wait_ge(sSum, k + 1)
                        gp.tensor_copy(hi[d][:], stg[k % 3][:, 0, 0:w]).then_inc(sHi, 1)

            @blk.vector
            def _(v):
                v.memset(ones_bf[:], 1.0)
                for k in range(NIT):
                    d, tp = k // 5, k % 5
                    w = 128 if tp >= 3 else D
                    v.wait_ge(sLd, 32 * (k + 1))
                    if not _SIM:
                        v.nop(cycle_cnt=3072 if k == 0 else 1024)
                    if tp == 2:
                        v.nop().then_inc(sSum, 1)
                        v.wait_ge(sHi, k + 1)
                        v.tensor_sub(wrl[d][:], stg[k % 3][:, 0, :], wrh[d][:])
                        v.tensor_sub(wil[d][:], stg[k % 3][:, 1, :], wih[d][:]).then_inc(sSplit, 1)
                    else:
                        v.tensor_add(stg[k % 3][:, 0, 0:w], stg[k % 3][:, 0, 0:w],
                                     stg[k % 3][:, 1, 0:w])
                        v.nop().then_inc(sSum, 1)
                        v.wait_ge(sHi, k + 1)
                        hi = [Chi, Shi, None, Cslh, Sslh][tp][d]
                        lo = [Clo, Slo, None, Csll, Ssll][tp][d]
                        if tp == 4:
                            v.tensor_sub(lo[:], stg[k % 3][:, 0, 0:w], hi[:])
                            v.tensor_scalar_mul(negSh[d][:], hi[:], -1.0)
                            v.scalar_tensor_tensor(
                                negSl[d][:], stg[k % 3][:, 0, 0:w], -1.0, hi[:],
                                ALU.mult, ALU.add).then_inc(sSplit, 1)
                        else:
                            v.tensor_sub(lo[:], stg[k % 3][:, 0, 0:w], hi[:]).then_inc(sSplit, 1)
                v.wait_ge(sP0, 1)
                for k in range(4):
                    v.tensor_copy(acc[k][:], P[k][:])
                    v.tensor_add(acc[k][:], acc[k][:], P[k + 4][:]).then_inc(sAdd, 1)
                for k in range(4):
                    hi = [AtUh, BtUh, AtUh, BtUh][k]
                    lo = [AtUl, BtUl, AtUl, BtUl][k]
                    h = k // 2
                    scl = (1.0 / D) if k % 2 == 0 else (-1.0 / D)
                    v.wait_ge(sEvH, k + 1)
                    v.scalar_tensor_tensor(
                        lo[:, 512 * h:512 * (h + 1)], acc[k][:], scl,
                        hi[:, 512 * h:512 * (h + 1)],
                        ALU.mult, ALU.subtract).then_inc(sEvL, 1)
                v.wait_ge(sP0, 2)
                v.tensor_copy(acc[0][:], P[0][:])
                v.tensor_add(acc[0][:], acc[0][:], P[1][:])
                v.tensor_add(acc[0][:], acc[0][:], P[4][:])
                v.tensor_add(acc[0][:], acc[0][:], P[5][:]).then_inc(sAdd, 1)
                v.tensor_copy(acc[2][:], P[2][:])
                v.tensor_add(acc[2][:], acc[2][:], P[3][:])
                v.tensor_add(acc[2][:], acc[2][:], P[6][:])
                v.tensor_add(acc[2][:], acc[2][:], P[7][:]).then_inc(sAdd, 1)
                for h in range(2):
                    v.wait_ge(sEvH, h + 5)
                    v.scalar_tensor_tensor(
                        ml_ev[:, 512 * h:512 * (h + 1)], acc[2 * h][:], 1.0,
                        mh_ev[:, 512 * h:512 * (h + 1)],
                        ALU.mult, ALU.subtract).then_inc(sEvL, 1)

            @blk.scalar
            def _(sc):
                for k in range(4):
                    hi = [AtUh, BtUh, AtUh, BtUh][k]
                    h = k // 2
                    sc.wait_ge(sAdd, k + 1)
                    sc.activation(hi[:, 512 * h:512 * (h + 1)], acc[k][:],
                                  AF.Copy,
                                  scale=(1.0 / D) if k % 2 == 0 else (-1.0 / D)
                                  ).then_inc(sEvH, 1)
                for h in range(2):
                    sc.wait_ge(sAdd, h + 5)
                    sc.activation(mh_ev[:, 512 * h:512 * (h + 1)], acc[2 * h][:],
                                  AF.Copy, scale=1.0).then_inc(sEvH, 1)

            @blk.tensor
            def _(t):
                def _mm(out, lt, rt, start, stop, gap=P0GAP):
                    t.matmul(out, lt, rt, start=start, stop=stop)
                    if not _SIM:
                        t.nop(cycle_cnt=gap)
                t.wait_ge(sSplit, NIT)
                for d in range(DCN):
                    q = d % 2
                    st = d < 2
                    sp = d >= DCN - 2
                    for ti in range(6):
                        la, ra = atu_terms(d)[ti]
                        lb, rb = btu_terms(d)[ti]
                        for h in range(2):
                            _mm(P[4 * q + 2 * h][:], la[:], ra[:, 512 * h:512 * (h + 1)],
                                start=(st and ti == 0), stop=(sp and ti == 5))
                            _mm(P[4 * q + 2 * h + 1][:], lb[:], rb[:, 512 * h:512 * (h + 1)],
                                start=(st and ti == 0), stop=(sp and ti == 5))
                t.nop().then_inc(sP0, 1)
                t.wait_ge(sTT, 64)
                for ti in range(6):
                    for ic in range(DCN):
                        lt, rt = msl_terms(ic)[ti]
                        st = (ti == 0 and ic < 4)
                        sp = (ti == 5 and ic >= 4)
                        _mm(P[[0, 1, 4, 5][ic % 4]][:], lt, rt[:, 0:512],
                            start=st, stop=sp, gap=P0GAPM)
                        _mm(P[[2, 3, 6, 7][ic % 4]][:], lt, rt[:, 512:1024],
                            start=st, stop=sp, gap=P0GAPM)
                t.nop().then_inc(sP0, 1)

        es_st.close()
        es_w.close()

        es_wb = contextlib.ExitStack()
        w_in_bf = [es_wb.enter_context(nc.sbuf_tensor(f"w_in_bf{d}", [128, 2 * INNER], BF16, side="right")) for d in range(DCN)]
        w_out_bf = [es_wb.enter_context(nc.sbuf_tensor(f"w_out_bf{c}", [128, D], BF16, side="right")) for c in range(CCN)]

        with nc.Block("p0b") as blk:
            @blk.gpsimd
            def _(gp):
                gp.wait_ge(sSplit, NIT)
                gp.dma_start(conv_w_sb[:], conv_w_h).then_inc(sWL, 16)
                gp.dma_start(conv_b_sb[:], conv_b_h).then_inc(sWL, 16)
                gp.dma_start(bias_bf[:], bias_h).then_inc(sWL, 16)
                for c in range(CCN):
                    gp.dma_start(w_out_bf[c][:], w_out_c[c]).then_inc(sWL, 16)
                gp.wait_ge(sP0, 1)
                for d in range(DCN):
                    gp.dma_start(w_in_bf[d][:], w_in_c[d]).then_inc(sWL, 16)
                gp.wait_ge(sStA, 32)
                gp.collective_compute(
                    "AllGather", ALU.bypass,
                    replica_groups=[list(range(N_CORES))],
                    ins=[mslice_hbm.opt()],
                    outs=[mfull_hbm.opt()],
                ).then_inc(sMB, 1)
        W_CONV = 16 * 2
        W_BIAS = 16 * 3
        W_WOUT = 16 * (3 + CCN)
        W_WIN = 16 * (3 + CCN + DCN)

        es0.close()
        es_p0.close()

        esA = contextlib.ExitStack()
        esA_p = contextlib.ExitStack()

        def sbA(name, shape, dtype):
            return esA.enter_context(nc.sbuf_tensor(name, shape, dtype))

        x_tm = [sbA(f"x_tm{s}", [128, TC, D], F32) for s in range(2)]
        x_bf = [sbA(f"x_bf{s}", [128, D], BF16) for s in range(2)]
        x_fm = [sbA(f"x_fm{s}", [128, DCN, T], BF16) for s in range(2)]
        sg = [sbA(f"sg{s}", [128, T], F32) for s in range(2)]
        glu = [sbA(f"glu{j}", [128, PAD + T], BF16) for j in range(NJ)]
        acc = [sbA(f"acc{s}", [128, T], F32) for s in range(2)]
        z1 = sbA("z1", [128, TC, D], F32)
        xh_tm = sbA("xh_tm", [128, TC, D], BF16)
        xl_tm = sbA("xl_tm", [128, TC, D], BF16)
        sqscr = sbA("sqscr", [128, D], BF16)
        sum1 = [sbA(f"sum1_{s}", [128, 4], F32) for s in range(2)]
        sumsq1 = [sbA(f"sumsq1_{s}", [128, 4], F32) for s in range(2)]
        stat1 = [sbA(f"stat1_{s}", [128, 16], F32) for s in range(2)]
        rb1 = [sbA(f"rb1_{s}", [128, 12], F32) for s in range(2)]
        pp = [esA_p.enter_context(nc.psum_tensor(f"pp{s}", [128, 1024], F32)) for s in range(4)]

        with nc.Block("pa") as blk:
            @blk.sync
            def _(sy):
                sy.wait_ge(sStA, 32)
                for i in range(NTILE):
                    if i >= 2:
                        sy.wait_ge(sZ1, 4 * (i - 1))
                    sy.dma_start(x_tm[i % 2][:], x_t[i]).then_inc(sX, 16)
                    for ts in range(TC):
                        tk = 4 * i + ts
                        sy.wait_ge(sXB, tk + 1)
                        if i >= 2 and ts == 0:
                            sy.wait_ge(sHa, NJ * (i - 1))
                        sy.dma_start(x_fm[i % 2][:, :, ts * 128:(ts + 1) * 128],
                                     x_bf[ts % 2][:], transpose=True).then_inc(sXF, 16)
                    if i >= 1:
                        sy.wait_ge(sXHc, 4 * i)
                        sy.dma_start(xh_t[i - 1], xh_tm[:]).then_inc(sSt, 16)
                        sy.wait_ge(sXL, 4 * i)
                        sy.dma_start(xl_t[i - 1], xl_tm[:]).then_inc(sSt, 16)
                sy.wait_ge(sXHc, 4 * NTILE)
                sy.dma_start(xh_t[NTILE - 1], xh_tm[:]).then_inc(sSt, 16)
                sy.wait_ge(sXL, 4 * NTILE)
                sy.dma_start(xl_t[NTILE - 1], xl_tm[:]).then_inc(sSt, 16)

            @blk.tensor
            def _(t):
                def _mm(*a, **k):
                    t.matmul(*a, **k)
                    if not _SIM:
                        t.nop(cycle_cnt=MMGAP)
                t.wait_ge(sWL, W_WIN)
                for i in range(NTILE):
                    for bh in range(NJ // 4):
                        js = [4 * bh + q for q in range(4)]
                        if bh == 0:
                            t.wait_ge(sXF, 16 * 4 * (i + 1))
                            if i >= 1:
                                t.wait_ge(sZ1, 4 * i)
                        for d in range(DCN):
                            for q in range(4):
                                j = js[q]
                                J = NJ * i + j
                                if d == 0 and J >= 4:
                                    t.wait_ge(sSG, J - 3)
                                _mm(pp[q][:, 0:512],
                                    w_in_bf[d][:, (NJ + j) * 128:(NJ + j + 1) * 128],
                                    x_fm[i % 2][:, d, :],
                                    start=(d == 0), stop=(d == DCN - 1))
                                if d == 0 and J >= 4:
                                    t.wait_ge(sGLU, J - 3)
                                _mm(pp[q][:, 512:1024],
                                    w_in_bf[d][:, j * 128:(j + 1) * 128],
                                    x_fm[i % 2][:, d, :],
                                    start=(d == 0), stop=(d == DCN - 1))
                        t.nop().then_inc(sHg, 4)
                        t.nop().then_inc(sHa, 4)
                    if i == 0:
                        t.wait_ge(sWL, W_WOUT)
                    t.wait_ge(sSI, NJ * (i + 1))
                    for c in range(CCN):
                        for ts in range(TC):
                            for db in range(2):
                                _mm(pp[ts][:, db * 512:(db + 1) * 512],
                                    glu[c][:, PAD + ts * 128:PAD + (ts + 1) * 128],
                                    w_out_bf[c][:, db * 512:(db + 1) * 512],
                                    start=(c == 0), stop=(c == CCN - 1))
                    t.nop().then_inc(sWO, 8)

            @blk.scalar
            def _(sc):
                for i in range(NTILE):
                    for ts in range(TC):
                        tk = 4 * i + ts
                        sc.wait_ge(sX, 16 * (i + 1))
                        if tk >= 2:
                            sc.wait_ge(sXF, 16 * (tk - 1))
                        sc.copy(x_bf[ts % 2][:], x_tm[i % 2][:, ts, :]).then_inc(sXB, 1)
                    if i >= 1:
                        for ts in range(TC):
                            z = 4 * (i - 1) + ts
                            sc.wait_ge(sX1, z + 1)
                            if i >= 2 and ts == 0:
                                sc.wait_ge(sSt, 32 * (i - 1))
                            sc.copy(xh_tm[:, ts, :], z1[:, ts, :]).then_inc(sXHc, 1)
                    for j in range(NJ):
                        J = NJ * i + j
                        sc.wait_ge(sHg, 4 * (4 * i + j // 4 + 1))
                        if J >= 2:
                            sc.wait_ge(sGLU, J - 1)
                        sc.activation(sg[j % 2][:], pp[j % 4][:, 0:512],
                                      AF.Sigmoid).then_inc(sSG, 1)
                        sc.wait_ge(sCV, J + 1)
                        sc.activation(glu[j][:, PAD:PAD + T], acc[j % 2][:],
                                      AF.Silu,
                                      bias=conv_b_sb[:, j:j + 1]).then_inc(sSI, 1)
                    for ts in range(TC):
                        z = 4 * i + ts
                        sc.wait_ge(sZ1, z + 1)
                        sc.activation(sqscr[:], z1[:, ts, :], AF.Square,
                                      accum_out=sumsq1[i % 2][:, ts:ts + 1]).then_inc(sSQ1, 1)
                    sc.wait_ge(sVE, i + 1)
                    sc.activation(rb1[i % 2][:, 8:12], stat1[i % 2][:, 8:12],
                                  AF.Sqrt).then_inc(sSD, 1)
                for ts in range(TC):
                    z = 4 * (NTILE - 1) + ts
                    sc.wait_ge(sX1, z + 1)
                    if ts == 0:
                        sc.wait_ge(sSt, 32 * (NTILE - 1))
                    sc.copy(xh_tm[:, ts, :], z1[:, ts, :]).then_inc(sXHc, 1)

            @blk.vector
            def _(v):
                vv_c = [0]
                v.wait_ge(sStA, 32)
                for i in range(NTILE):
                    if i % SEQ_TILES == 0:
                        if i > 0:
                            v.wait_ge(sWO, 8 * i)
                        for j in range(NJ):
                            v.memset(glu[j][:, 1:PAD], 0.0)
                    for j in range(NJ):
                        J = NJ * i + j
                        v.wait_ge(sSG, J + 1)
                        v.wait_ge(sHa, 4 * (4 * i + j // 4 + 1))
                        if j == 0 and i > 0 and i % SEQ_TILES != 0:
                            v.wait_ge(sWO, 8 * i)
                        if j == 0 and i == 0:
                            v.wait_ge(sWL, W_CONV)
                        v.tensor_mul(glu[j][:, PAD:PAD + T],
                                     pp[j % 4][:, 512:1024], sg[j % 2][:]).then_inc(sGLU, 1)
                        if J >= 2:
                            v.wait_ge(sSI, J - 1)
                        v.tensor_scalar_mul(acc[j % 2][:], glu[j][:, 1:1 + T],
                                            conv_w_sb[:, j, 0:1])
                        for k in range(1, 4):
                            v.scalar_tensor_tensor(acc[j % 2][:], glu[j][:, 1 + k:1 + k + T],
                                                   conv_w_sb[:, j, k:k + 1],
                                                   acc[j % 2][:], ALU.mult, ALU.add)
                        v.tensor_copy(glu[j][:, 1:PAD], glu[j][:, 1 + T:PAD + T]).then_inc(sCV, 1)
                    for ts in range(TC):
                        z = 4 * i + ts
                        v.wait_ge(sWO, 8 * (i + 1))
                        if ts == 0:
                            v.wait_ge(sX, 16 * (i + 1))
                        if z >= 4:
                            v.wait_ge(sSQ1, z - 3)
                            v.wait_ge(sXHc, z - 3)
                        v.scalar_tensor_tensor(z1[:, ts, :], pp[ts][:], 1.0,
                                               x_tm[i % 2][:, ts, :], ALU.mult, ALU.add,
                                               accum_out=sum1[i % 2][:, ts:ts + 1]).then_inc(sZ1, 1)
                    v.wait_ge(sSQ1, 4 * (i + 1))
                    v.wait_ge(sZ1, 4 * (i + 1))
                    st = stat1[i % 2]
                    v.tensor_scalar_mul(st[:, 0:4], sum1[i % 2][:], -1.0 / D)
                    v.tensor_scalar(st[:, 4:8], sumsq1[i % 2][:], 1.0 / D, EPS,
                                    ALU.mult, ALU.add)
                    v.tensor_scalar_mul(st[:, 8:12], sum1[i % 2][:],
                                        1.0 / D).then_inc(sVV, 1)
                    vv_c[0] += 1
                    v.wait_ge(sVV, vv_c[0])
                    v.tensor_mul(st[:, 12:16], st[:, 8:12], st[:, 0:4]).then_inc(sVV, 1)
                    vv_c[0] += 1
                    v.wait_ge(sVV, vv_c[0])
                    v.tensor_add(st[:, 8:12], st[:, 4:8], st[:, 12:16]).then_inc(sVE, 1)
                    v.wait_ge(sSD, i + 1)
                    v.reciprocal(rb1[i % 2][:, 0:4], rb1[i % 2][:, 8:12]).then_inc(sVV, 1)
                    vv_c[0] += 1
                    v.wait_ge(sVV, vv_c[0])
                    v.tensor_mul(rb1[i % 2][:, 4:8], st[:, 0:4],
                                 rb1[i % 2][:, 0:4]).then_inc(sVV, 1)
                    vv_c[0] += 1
                    v.wait_ge(sVV, vv_c[0])
                    for ts in range(TC):
                        z = 4 * i + ts
                        v.tensor_scalar_mul(z1[:, ts, :], z1[:, ts, :],
                                            rb1[i % 2][:, ts:ts + 1])
                        v.tensor_scalar_add(z1[:, ts, :], z1[:, ts, :],
                                            rb1[i % 2][:, 4 + ts:5 + ts]).then_inc(sX1, 1)
                    for ts in range(TC):
                        z = 4 * i + ts
                        v.wait_ge(sXHc, z + 1)
                        if ts == 0 and i >= 1:
                            v.wait_ge(sSt, 32 * i)
                        v.tensor_sub(xl_tm[:, ts, :], z1[:, ts, :],
                                     xh_tm[:, ts, :]).then_inc(sXL, 1)

        esA.close()
        esA_p.close()
        es_wb.close()

        esB = contextlib.ExitStack()
        esB_p = contextlib.ExitStack()

        def sbB(name, shape, dtype):
            return esB.enter_context(nc.sbuf_tensor(name, shape, dtype))

        Mh2 = sbB("Mh2", [128, DCN, D], BF16)
        Ml2 = sbB("Ml2", [128, DCN, D], BF16)
        xh2 = [sbB(f"xh2_{s}", [128, TC, D], BF16) for s in range(2)]
        xl2 = [sbB(f"xl2_{s}", [128, TC, D], BF16) for s in range(2)]
        xfh = [sbB(f"xfh{s}", [128, DCN, T], BF16) for s in range(2)]
        xfl = [sbB(f"xfl{s}", [128, DCN, T], BF16) for s in range(2)]
        z2 = sbB("z2", [128, TC, D], F32)
        outb = [sbB(f"outb{s}", [128, TC, D], F32) for s in range(2)]
        sqscr2 = sbB("sqscr2", [128, D], BF16)
        sum2 = [sbB(f"sum2_{s}", [128, 4], F32) for s in range(2)]
        sumsq2 = [sbB(f"sumsq2_{s}", [128, 4], F32) for s in range(2)]
        stat2 = [sbB(f"stat2_{s}", [128, 16], F32) for s in range(2)]
        rb2 = [sbB(f"rb2_{s}", [128, 12], F32) for s in range(2)]
        pf = [esB_p.enter_context(nc.psum_tensor(f"pf{s}", [128, D], F32)) for s in range(4)]

        with nc.Block("pb") as blk:
            @blk.sync
            def _(sy):
                sy.wait_ge(sSt, 32 * NTILE)
                sy.wait_ge(sMB, 1)
                sy.dma_start(Mh2[:], mfull_h[0]).then_inc(sMB, 16)
                sy.dma_start(Ml2[:], mfull_h[1]).then_inc(sMB, 16)
                for i in range(NTILE):
                    if i >= 2:
                        sy.wait_ge(sZ2, 4 * (i - 1))
                    sy.dma_start(xh2[i % 2][:], xh_t[i]).then_inc(sBX, 16)
                    sy.dma_start(xl2[i % 2][:], xl_t[i]).then_inc(sBX, 16)
                    if i >= 1:
                        sy.wait_ge(sOut, 4 * i)
                        sy.dma_start(out_t[i - 1], outb[(i - 1) % 2][:]).then_inc(sOS, 16)
                sy.wait_ge(sOut, 4 * NTILE)
                sy.dma_start(out_t[NTILE - 1], outb[(NTILE - 1) % 2][:]).then_inc(sOS, 16)
                sy.wait_ge(sOS, 16 * NTILE)

            @blk.scalar
            def _(sc):
                for i in range(NTILE):
                    for ts in range(TC):
                        sc.wait_ge(sBX, 32 * (i + 1))
                        if i >= 2 and ts == 0:
                            sc.wait_ge(sFB, 8 * (i - 1))
                        sc.dma_start(xfh[i % 2][:, :, ts * 128:(ts + 1) * 128],
                                     xh2[i % 2][:, ts, :], transpose=True).then_inc(sBF, 16)
                        sc.dma_start(xfl[i % 2][:, :, ts * 128:(ts + 1) * 128],
                                     xl2[i % 2][:, ts, :], transpose=True).then_inc(sBF, 16)
                    for ts in range(TC):
                        z = 4 * i + ts
                        sc.wait_ge(sZ2, z + 1)
                        sc.activation(sqscr2[:], z2[:, ts, :], AF.Square,
                                      accum_out=sumsq2[i % 2][:, ts:ts + 1]).then_inc(sSQ2, 1)
                    sc.wait_ge(sVE2, i + 1)
                    sc.activation(rb2[i % 2][:, 8:12], stat2[i % 2][:, 8:12],
                                  AF.Sqrt).then_inc(sSD2, 1)

            @blk.tensor
            def _(t):
                def _mm(*a, **k):
                    t.matmul(*a, **k)
                    if not _SIM:
                        t.nop(cycle_cnt=MMGAP)
                t.wait_ge(sWL, W_BIAS)
                t.wait_ge(sMB, 33)
                for i in range(NTILE):
                    t.wait_ge(sBF, 16 * 8 * (i + 1))
                    if i >= 1:
                        t.wait_ge(sZ2, 4 * i)
                    for step in range(25):
                        for ts in range(TC):
                            for ob in range(2):
                                psl = pf[ts][:, ob * 512:(ob + 1) * 512]
                                if step == 0:
                                    _mm(psl, ones_bf[:],
                                        bias_bf[:, ob * 512:(ob + 1) * 512],
                                        start=True, stop=False)
                                else:
                                    s = step - 1
                                    ic, which = s // 3, s % 3
                                    lx = xfh[i % 2][:, ic, ts * 128:(ts + 1) * 128]
                                    ll = xfl[i % 2][:, ic, ts * 128:(ts + 1) * 128]
                                    if which == 0:
                                        _mm(psl, lx, Mh2[:, ic, ob * 512:(ob + 1) * 512],
                                            start=False, stop=False)
                                    elif which == 1:
                                        _mm(psl, lx, Ml2[:, ic, ob * 512:(ob + 1) * 512],
                                            start=False, stop=False)
                                    else:
                                        _mm(psl, ll, Mh2[:, ic, ob * 512:(ob + 1) * 512],
                                            start=False, stop=(step == 24))
                    t.nop().then_inc(sFB, 8)

            @blk.vector
            def _(v):
                vv_c = [32]
                for i in range(NTILE):
                    for ts in range(TC):
                        z = 4 * i + ts
                        v.wait_ge(sFB, 8 * (i + 1))
                        if z >= 4:
                            v.wait_ge(sSQ2, z - 3)
                        v.tensor_add(z2[:, ts, :], pf[ts][:], xh2[i % 2][:, ts, :])
                        v.scalar_tensor_tensor(z2[:, ts, :], xl2[i % 2][:, ts, :], 1.0,
                                               z2[:, ts, :], ALU.mult, ALU.add,
                                               accum_out=sum2[i % 2][:, ts:ts + 1]).then_inc(sZ2, 1)
                    v.wait_ge(sSQ2, 4 * (i + 1))
                    v.wait_ge(sZ2, 4 * (i + 1))
                    st = stat2[i % 2]
                    v.tensor_scalar_mul(st[:, 0:4], sum2[i % 2][:], -1.0 / D)
                    v.tensor_scalar(st[:, 4:8], sumsq2[i % 2][:], 1.0 / D, EPS,
                                    ALU.mult, ALU.add)
                    v.tensor_scalar_mul(st[:, 8:12], sum2[i % 2][:],
                                        1.0 / D).then_inc(sVV, 1)
                    vv_c[0] += 1
                    v.wait_ge(sVV, vv_c[0])
                    v.tensor_mul(st[:, 12:16], st[:, 8:12], st[:, 0:4]).then_inc(sVV, 1)
                    vv_c[0] += 1
                    v.wait_ge(sVV, vv_c[0])
                    v.tensor_add(st[:, 8:12], st[:, 4:8], st[:, 12:16]).then_inc(sVE2, 1)
                    v.wait_ge(sSD2, i + 1)
                    v.reciprocal(rb2[i % 2][:, 0:4], rb2[i % 2][:, 8:12]).then_inc(sVV, 1)
                    vv_c[0] += 1
                    v.wait_ge(sVV, vv_c[0])
                    v.tensor_mul(rb2[i % 2][:, 4:8], st[:, 0:4],
                                 rb2[i % 2][:, 0:4]).then_inc(sVV, 1)
                    vv_c[0] += 1
                    v.wait_ge(sVV, vv_c[0])
                    for ts in range(TC):
                        z = 4 * i + ts
                        if i >= 2 and ts == 0:
                            v.wait_ge(sOS, 16 * (i - 1))
                        v.tensor_scalar_mul(outb[i % 2][:, ts, :], z2[:, ts, :],
                                            rb2[i % 2][:, ts:ts + 1])
                        v.tensor_scalar_add(outb[i % 2][:, ts, :], outb[i % 2][:, ts, :],
                                            rb2[i % 2][:, 4 + ts:5 + ts]).then_inc(sOut, 1)

            @blk.gpsimd
            def _(gp):
                gp.engine_nop()

        esB.close()
        esB_p.close()
    return nc


def _host_prepare(inputs):
    x = np.ascontiguousarray(np.asarray(inputs["L0_attn_x"], dtype=np.float32))
    w_in = np.ascontiguousarray(np.asarray(inputs["L0_attn_W_in"], dtype=np.float32))
    conv_w = np.asarray(inputs["L0_attn_conv_w"], dtype=np.float32)
    conv_b = np.asarray(inputs["L0_attn_conv_b"], dtype=np.float32)
    w_out = np.ascontiguousarray(np.asarray(inputs["L0_attn_W_out"], dtype=np.float32))
    wr = np.ascontiguousarray(np.asarray(inputs["L0_ffn_w_freq_real"], dtype=np.float32))
    wi = np.ascontiguousarray(np.asarray(inputs["L0_ffn_w_freq_imag"], dtype=np.float32))
    bias = np.ascontiguousarray(np.asarray(inputs["L0_ffn_bias"], dtype=np.float32).reshape(1, D))

    conv_w_r = np.ascontiguousarray(conv_w.reshape(CCN, 128, 4).transpose(1, 0, 2))
    conv_b_r = np.ascontiguousarray(conv_b.reshape(CCN, 128).T)

    n = np.arange(D, dtype=np.float64)
    ang = (2.0 * np.pi / D) * np.outer(n, n)
    c64 = np.cos(ang)
    s64 = np.sin(ang)
    ca = c64.astype(np.float32)
    cb = (c64 - ca.astype(np.float64)).astype(np.float32)
    sa = s64.astype(np.float32)
    sb_ = (s64 - sa.astype(np.float64)).astype(np.float32)

    common = {
        "w_in": w_in, "conv_w": conv_w_r, "conv_b": conv_b_r, "w_out": w_out,
        "wr": wr, "wi": wi, "ca": np.ascontiguousarray(ca),
        "cb": np.ascontiguousarray(cb), "sa": np.ascontiguousarray(sa),
        "sb": np.ascontiguousarray(sb_), "bias": bias,
    }
    B = x.shape[0]
    per = B // N_CORES
    in_maps = []
    for c in range(N_CORES):
        m = dict(common)
        m["x"] = np.ascontiguousarray(x[c * per:(c + 1) * per].reshape(NTOK, D))
        sl = slice(128 * c, 128 * (c + 1))
        m["cas"] = np.ascontiguousarray(ca[:, sl])
        m["cbs"] = np.ascontiguousarray(cb[:, sl])
        m["sas"] = np.ascontiguousarray(sa[:, sl])
        m["sbs"] = np.ascontiguousarray(sb_[:, sl])
        in_maps.append(m)
    return in_maps, x.shape


def kernel(L0_attn_x, L0_attn_W_in, L0_attn_conv_w, L0_attn_conv_b,
           L0_attn_W_out, L0_ffn_w_freq_real, L0_ffn_w_freq_imag, L0_ffn_bias):
    inputs = {
        "L0_attn_x": L0_attn_x, "L0_attn_W_in": L0_attn_W_in,
        "L0_attn_conv_w": L0_attn_conv_w, "L0_attn_conv_b": L0_attn_conv_b,
        "L0_attn_W_out": L0_attn_W_out, "L0_ffn_w_freq_real": L0_ffn_w_freq_real,
        "L0_ffn_w_freq_imag": L0_ffn_w_freq_imag, "L0_ffn_bias": L0_ffn_bias,
    }
    in_maps, xshape = _host_prepare(inputs)
    nc = build_program()
    res = run_bass_kernel_spmd(nc, in_maps, core_ids=list(range(N_CORES)))
    per = xshape[0] // N_CORES
    out = np.empty(xshape, dtype=np.float32)
    for c in range(N_CORES):
        out[c * per:(c + 1) * per] = np.asarray(
            res.results[c]["out"]).reshape(per, xshape[1], xshape[2])
    return out


# version 20 (final)
# speedup vs baseline: 1.0469x; 1.0317x over previous
import contextlib
import os
import numpy as np
import concourse.bass as bass
import concourse.mybir as mybir
from concourse.bass_utils import run_bass_kernel_spmd

F32 = mybir.dt.float32
BF16 = mybir.dt.bfloat16
AF = mybir.ActivationFunctionType
ALU = mybir.AluOpType

D = 1024
INNER = 2048
NTOK = 4096
T = 512
NTILE = NTOK // T
TC = T // 128
SEQ_TILES = 2048 // T
DCN = D // 128
CCN = INNER // 128
NJ = 16
PAD = 4
EPS = 1e-5
N_CORES = 8
MMGAP = 224
P0GAP = 512
P0GAPM = 512
_SIM = os.environ.get("BASS_SIM_FP64") == "1"


def build_program():
    nc = bass.Bass("TRN2", target_bir_lowering=False,
                   detect_race_conditions=False,
                   dynamic_dma_scratch_size=4096)

    x_in = nc.dram_tensor("x", [NTOK, D], F32, kind="ExternalInput").ap()
    w_in_h = nc.dram_tensor("w_in", [D, 2 * INNER], F32, kind="ExternalInput").ap()
    conv_w_h = nc.dram_tensor("conv_w", [128, CCN, 4], F32, kind="ExternalInput").ap()
    conv_b_h = nc.dram_tensor("conv_b", [128, CCN], F32, kind="ExternalInput").ap()
    w_out_h = nc.dram_tensor("w_out", [INNER, D], F32, kind="ExternalInput").ap()
    wr_h = nc.dram_tensor("wr", [D, D], F32, kind="ExternalInput").ap()
    wi_h = nc.dram_tensor("wi", [D, D], F32, kind="ExternalInput").ap()
    ca_h = nc.dram_tensor("ca", [D, D], F32, kind="ExternalInput").ap()
    cb_h = nc.dram_tensor("cb", [D, D], F32, kind="ExternalInput").ap()
    sa_h = nc.dram_tensor("sa", [D, D], F32, kind="ExternalInput").ap()
    sb_h = nc.dram_tensor("sb", [D, D], F32, kind="ExternalInput").ap()
    cas_h = nc.dram_tensor("cas", [D, 128], F32, kind="ExternalInput").ap()
    cbs_h = nc.dram_tensor("cbs", [D, 128], F32, kind="ExternalInput").ap()
    sas_h = nc.dram_tensor("sas", [D, 128], F32, kind="ExternalInput").ap()
    sbs_h = nc.dram_tensor("sbs", [D, 128], F32, kind="ExternalInput").ap()
    bias_h = nc.dram_tensor("bias", [1, D], F32, kind="ExternalInput").ap()
    out_h = nc.dram_tensor("out", [NTOK, D], F32, kind="ExternalOutput").ap()
    xh_hbm = nc.dram_tensor("xh_scr", [NTOK, D], BF16).ap()
    xl_hbm = nc.dram_tensor("xl_scr", [NTOK, D], BF16).ap()
    mslice_hbm = nc.dram_tensor("m_slice", [256, D], BF16).ap()
    mfull_hbm = nc.dram_tensor("m_full", [8 * 256, D], BF16, addr_space="Shared").ap()

    x_t = x_in.rearrange("(n c p) d -> n p c d", c=TC, p=128)
    out_t = out_h.rearrange("(n c p) d -> n p c d", c=TC, p=128)
    xh_t = xh_hbm.rearrange("(n c p) d -> n p c d", c=TC, p=128)
    xl_t = xl_hbm.rearrange("(n c p) d -> n p c d", c=TC, p=128)
    w_in_c = w_in_h.rearrange("(k p) f -> k p f", p=128)
    w_out_c = w_out_h.rearrange("(k p) f -> k p f", p=128)
    wr_c = wr_h.rearrange("(k p) f -> k p f", p=128)
    wi_c = wi_h.rearrange("(k p) f -> k p f", p=128)
    ca_c = ca_h.rearrange("(k p) f -> k p f", p=128)
    cb_c = cb_h.rearrange("(k p) f -> k p f", p=128)
    sa_c = sa_h.rearrange("(k p) f -> k p f", p=128)
    sb_c = sb_h.rearrange("(k p) f -> k p f", p=128)
    cas_c = cas_h.rearrange("(k p) f -> k p f", p=128)
    cbs_c = cbs_h.rearrange("(k p) f -> k p f", p=128)
    sas_c = sas_h.rearrange("(k p) f -> k p f", p=128)
    sbs_c = sbs_h.rearrange("(k p) f -> k p f", p=128)
    mfull_h = mfull_hbm.rearrange("(c t p) f -> t p c f", c=8, t=2, p=128)

    es = contextlib.ExitStack()

    def sem(name):
        return es.enter_context(nc.semaphore(name))

    with es:
        ones_bf = es.enter_context(nc.sbuf_tensor("ones_bf", [1, 128], BF16))
        bias_bf = es.enter_context(nc.sbuf_tensor("bias_bf", [1, D], BF16))
        conv_w_sb = es.enter_context(nc.sbuf_tensor("conv_w_sb", [128, CCN, 4], F32))
        conv_b_sb = es.enter_context(nc.sbuf_tensor("conv_b_sb", [128, CCN], F32))

        sLd = sem("sLd"); sSum = sem("sSum"); sHi = sem("sHi"); sSplit = sem("sSplit")
        sP0 = sem("sP0"); sEvH = sem("sEvH"); sEvL = sem("sEvL"); sNeg = sem("sNeg")
        sStA = sem("sStA"); sMl = sem("sMl"); sMB = sem("sMB")
        sWL = sem("sWL")
        sX = sem("sX"); sXB = sem("sXB"); sXF = sem("sXF")
        sHg = sem("sHg"); sHa = sem("sHa"); sSG = sem("sSG"); sGLU = sem("sGLU")
        sCV = sem("sCV"); sSI = sem("sSI"); sWO = sem("sWO"); sZ1 = sem("sZ1")
        sSQ1 = sem("sSQ1"); sVE = sem("sVE"); sSD = sem("sSD")
        sX1 = sem("sX1"); sXHc = sem("sXHc"); sXL = sem("sXL"); sSt = sem("sSt")
        sBX = sem("sBX"); sBF = sem("sBF"); sFB = sem("sFB"); sZ2 = sem("sZ2")
        sSQ2 = sem("sSQ2"); sVE2 = sem("sVE2"); sSD2 = sem("sSD2")
        sOut = sem("sOut"); sOS = sem("sOS"); sVV = sem("sVV"); sTT = sem("sTT"); sAdd = sem("sAdd"); sCp = sem("sCp")

        es0 = contextlib.ExitStack()
        es_w = contextlib.ExitStack()
        es_st = contextlib.ExitStack()
        es_p0 = contextlib.ExitStack()

        def sb0(name, shape, dtype):
            return es0.enter_context(nc.sbuf_tensor(name, shape, dtype))

        Chi = [sb0(f"Chi{d}", [128, D], BF16) for d in range(DCN)]
        Clo = [sb0(f"Clo{d}", [128, D], BF16) for d in range(DCN)]
        Shi = [sb0(f"Shi{d}", [128, D], BF16) for d in range(DCN)]
        Slo = [sb0(f"Slo{d}", [128, D], BF16) for d in range(DCN)]
        Cslh = [sb0(f"Cslh{d}", [128, 128], BF16) for d in range(DCN)]
        Csll = [sb0(f"Csll{d}", [128, 128], BF16) for d in range(DCN)]
        Sslh = [sb0(f"Sslh{d}", [128, 128], BF16) for d in range(DCN)]
        Ssll = [sb0(f"Ssll{d}", [128, 128], BF16) for d in range(DCN)]
        negSh = [sb0(f"negSh{d}", [128, 128], BF16) for d in range(DCN)]
        negSl = [sb0(f"negSl{d}", [128, 128], BF16) for d in range(DCN)]
        AtUh = sb0("AtUh", [128, D], BF16)
        AtUl = sb0("AtUl", [128, D], BF16)
        BtUh = sb0("BtUh", [128, D], BF16)
        BtUl = sb0("BtUl", [128, D], BF16)
        AsTh = sb0("AsTh", [128, DCN, 128], BF16)
        AsTl = sb0("AsTl", [128, DCN, 128], BF16)
        BsTh = sb0("BsTh", [128, DCN, 128], BF16)
        BsTl = sb0("BsTl", [128, DCN, 128], BF16)
        mh_ev = sb0("mh_ev", [128, D], BF16)
        ml_ev = sb0("ml_ev", [128, D], BF16)
        wrh = [es_w.enter_context(nc.sbuf_tensor(f"wrh{d}", [128, D], BF16, side="right")) for d in range(DCN)]
        wrl = [es_w.enter_context(nc.sbuf_tensor(f"wrl{d}", [128, D], BF16, side="right")) for d in range(DCN)]
        wih = [es_w.enter_context(nc.sbuf_tensor(f"wih{d}", [128, D], BF16, side="right")) for d in range(DCN)]
        wil = [es_w.enter_context(nc.sbuf_tensor(f"wil{d}", [128, D], BF16, side="right")) for d in range(DCN)]
        stg = [es_st.enter_context(nc.sbuf_tensor(f"stg{s}", [128, 2, D], F32)) for s in range(3)]
        acc = [es_st.enter_context(nc.sbuf_tensor(f"pacc{s}", [128, 512], F32)) for s in range(4)]
        P = [es_p0.enter_context(nc.psum_tensor(f"P{s}", [128, 512], F32)) for s in range(8)]

        def atu_terms(d):
            return [(Cslh[d], wrh[d]), (Cslh[d], wrl[d]), (Csll[d], wrh[d]),
                    (Sslh[d], wih[d]), (Sslh[d], wil[d]), (Ssll[d], wih[d])]

        def btu_terms(d):
            return [(Cslh[d], wih[d]), (Cslh[d], wil[d]), (Csll[d], wih[d]),
                    (negSh[d], wrh[d]), (negSh[d], wrl[d]), (negSl[d], wrh[d])]

        def msl_terms(ic):
            return [(AsTh[:, ic, :], Chi[ic]), (AsTh[:, ic, :], Clo[ic]),
                    (AsTl[:, ic, :], Chi[ic]), (BsTh[:, ic, :], Shi[ic]),
                    (BsTh[:, ic, :], Slo[ic]), (BsTl[:, ic, :], Shi[ic])]

        NIT = 40
        _item_srcs = lambda d: [(ca_c[d], cb_c[d]), (sa_c[d], sb_c[d]),
                                (wr_c[d], wi_c[d]), (cas_c[d], cbs_c[d]),
                                (sas_c[d], sbs_c[d])]

        with nc.Block("p0") as blk:
            @blk.sync
            def _(sy):
                for k in range(NIT):
                    if k >= 3:
                        sy.wait_ge(sSplit, k - 2)
                    d, tp = k // 5, k % 5
                    s0, s1 = _item_srcs(d)[tp]
                    w = 128 if tp >= 3 else D
                    sy.dma_start(stg[k % 3][:, 0, 0:w], s0).then_inc(sLd, 16)
                    sy.dma_start(stg[k % 3][:, 1, 0:w], s1).then_inc(sLd, 16)
                sy.wait_ge(sEvL, 3)
                sy.dma_start(AsTh[:], AtUh[:], transpose=True).then_inc(sTT, 16)
                sy.dma_start(AsTl[:], AtUl[:], transpose=True).then_inc(sTT, 16)
                sy.wait_ge(sEvL, 4)
                sy.dma_start(BsTh[:], BtUh[:], transpose=True).then_inc(sTT, 16)
                sy.dma_start(BsTl[:], BtUl[:], transpose=True).then_inc(sTT, 16)
                sy.wait_ge(sEvL, 6)
                sy.dma_start(mslice_hbm[0:128, :], mh_ev[:]).then_inc(sStA, 16)
                sy.dma_start(mslice_hbm[128:256, :], ml_ev[:]).then_inc(sStA, 16)

            @blk.gpsimd
            def _(gp):
                for k in range(NIT):
                    d, tp = k // 5, k % 5
                    w = 128 if tp >= 3 else D
                    hi = [Chi, Shi, None, Cslh, Sslh][tp]
                    if tp == 2:
                        gp.wait_ge(sLd, 32 * (k + 1))
                        gp.tensor_copy(wrh[d][:], stg[k % 3][:, 0, :])
                        gp.tensor_copy(wih[d][:], stg[k % 3][:, 1, :]).then_inc(sHi, 1)
                    else:
                        gp.wait_ge(sSum, k + 1)
                        gp.tensor_copy(hi[d][:], stg[k % 3][:, 0, 0:w]).then_inc(sHi, 1)

            @blk.vector
            def _(v):
                v.memset(ones_bf[:], 1.0)
                for k in range(NIT):
                    d, tp = k // 5, k % 5
                    w = 128 if tp >= 3 else D
                    v.wait_ge(sLd, 32 * (k + 1))
                    if not _SIM:
                        v.nop(cycle_cnt=3072 if k == 0 else 1024)
                    if tp == 2:
                        v.nop().then_inc(sSum, 1)
                        v.wait_ge(sHi, k + 1)
                        v.tensor_sub(wrl[d][:], stg[k % 3][:, 0, :], wrh[d][:])
                        v.tensor_sub(wil[d][:], stg[k % 3][:, 1, :], wih[d][:]).then_inc(sSplit, 1)
                    else:
                        v.tensor_add(stg[k % 3][:, 0, 0:w], stg[k % 3][:, 0, 0:w],
                                     stg[k % 3][:, 1, 0:w])
                        v.nop().then_inc(sSum, 1)
                        v.wait_ge(sHi, k + 1)
                        hi = [Chi, Shi, None, Cslh, Sslh][tp][d]
                        lo = [Clo, Slo, None, Csll, Ssll][tp][d]
                        if tp == 4:
                            v.tensor_sub(lo[:], stg[k % 3][:, 0, 0:w], hi[:])
                            v.tensor_scalar_mul(negSh[d][:], hi[:], -1.0)
                            v.scalar_tensor_tensor(
                                negSl[d][:], stg[k % 3][:, 0, 0:w], -1.0, hi[:],
                                ALU.mult, ALU.add).then_inc(sSplit, 1)
                        else:
                            v.tensor_sub(lo[:], stg[k % 3][:, 0, 0:w], hi[:]).then_inc(sSplit, 1)
                v.wait_ge(sP0, 1)
                for k in range(4):
                    v.tensor_copy(acc[k][:], P[k][:])
                    v.tensor_add(acc[k][:], acc[k][:], P[k + 4][:]).then_inc(sAdd, 1)
                for k in range(4):
                    hi = [AtUh, BtUh, AtUh, BtUh][k]
                    lo = [AtUl, BtUl, AtUl, BtUl][k]
                    h = k // 2
                    scl = (1.0 / D) if k % 2 == 0 else (-1.0 / D)
                    v.wait_ge(sEvH, k + 1)
                    v.scalar_tensor_tensor(
                        lo[:, 512 * h:512 * (h + 1)], acc[k][:], scl,
                        hi[:, 512 * h:512 * (h + 1)],
                        ALU.mult, ALU.subtract).then_inc(sEvL, 1)
                v.wait_ge(sP0, 2)
                v.tensor_copy(acc[0][:], P[0][:])
                v.tensor_add(acc[0][:], acc[0][:], P[1][:])
                v.tensor_add(acc[0][:], acc[0][:], P[4][:])
                v.tensor_add(acc[0][:], acc[0][:], P[5][:]).then_inc(sAdd, 1)
                v.tensor_copy(acc[2][:], P[2][:])
                v.tensor_add(acc[2][:], acc[2][:], P[3][:])
                v.tensor_add(acc[2][:], acc[2][:], P[6][:])
                v.tensor_add(acc[2][:], acc[2][:], P[7][:]).then_inc(sAdd, 1)
                for h in range(2):
                    v.wait_ge(sEvH, h + 5)
                    v.scalar_tensor_tensor(
                        ml_ev[:, 512 * h:512 * (h + 1)], acc[2 * h][:], 1.0,
                        mh_ev[:, 512 * h:512 * (h + 1)],
                        ALU.mult, ALU.subtract).then_inc(sEvL, 1)

            @blk.scalar
            def _(sc):
                for k in range(4):
                    hi = [AtUh, BtUh, AtUh, BtUh][k]
                    h = k // 2
                    sc.wait_ge(sAdd, k + 1)
                    sc.activation(hi[:, 512 * h:512 * (h + 1)], acc[k][:],
                                  AF.Copy,
                                  scale=(1.0 / D) if k % 2 == 0 else (-1.0 / D)
                                  ).then_inc(sEvH, 1)
                for h in range(2):
                    sc.wait_ge(sAdd, h + 5)
                    sc.activation(mh_ev[:, 512 * h:512 * (h + 1)], acc[2 * h][:],
                                  AF.Copy, scale=1.0).then_inc(sEvH, 1)

            @blk.tensor
            def _(t):
                def _mm(out, lt, rt, start, stop, gap=P0GAP):
                    t.matmul(out, lt, rt, start=start, stop=stop)
                    if not _SIM:
                        t.nop(cycle_cnt=gap)
                t.wait_ge(sSplit, NIT)
                for d in range(DCN):
                    q = d % 2
                    st = d < 2
                    sp = d >= DCN - 2
                    for ti in range(6):
                        la, ra = atu_terms(d)[ti]
                        lb, rb = btu_terms(d)[ti]
                        for h in range(2):
                            _mm(P[4 * q + 2 * h][:], la[:], ra[:, 512 * h:512 * (h + 1)],
                                start=(st and ti == 0), stop=(sp and ti == 5))
                            _mm(P[4 * q + 2 * h + 1][:], lb[:], rb[:, 512 * h:512 * (h + 1)],
                                start=(st and ti == 0), stop=(sp and ti == 5))
                t.nop().then_inc(sP0, 1)
                t.wait_ge(sTT, 64)
                for ti in range(6):
                    for ic in range(DCN):
                        lt, rt = msl_terms(ic)[ti]
                        st = (ti == 0 and ic < 4)
                        sp = (ti == 5 and ic >= 4)
                        _mm(P[[0, 1, 4, 5][ic % 4]][:], lt, rt[:, 0:512],
                            start=st, stop=sp, gap=P0GAPM)
                        _mm(P[[2, 3, 6, 7][ic % 4]][:], lt, rt[:, 512:1024],
                            start=st, stop=sp, gap=P0GAPM)
                t.nop().then_inc(sP0, 1)

        es_st.close()
        es_w.close()

        es_wb = contextlib.ExitStack()
        w_in_bf = [es_wb.enter_context(nc.sbuf_tensor(f"w_in_bf{d}", [128, 2 * INNER], BF16, side="right")) for d in range(DCN)]
        w_out_bf = [es_wb.enter_context(nc.sbuf_tensor(f"w_out_bf{c}", [128, D], BF16, side="right")) for c in range(CCN)]

        with nc.Block("p0b") as blk:
            @blk.gpsimd
            def _(gp):
                gp.wait_ge(sSplit, NIT)
                gp.dma_start(conv_w_sb[:], conv_w_h).then_inc(sWL, 16)
                gp.dma_start(conv_b_sb[:], conv_b_h).then_inc(sWL, 16)
                gp.dma_start(bias_bf[:], bias_h).then_inc(sWL, 16)
                for c in range(CCN):
                    gp.dma_start(w_out_bf[c][:], w_out_c[c]).then_inc(sWL, 16)
                gp.wait_ge(sP0, 1)
                for d in range(DCN):
                    gp.dma_start(w_in_bf[d][:], w_in_c[d]).then_inc(sWL, 16)
                gp.wait_ge(sStA, 32)
                gp.collective_compute(
                    "AllGather", ALU.bypass,
                    replica_groups=[list(range(N_CORES))],
                    ins=[mslice_hbm.opt()],
                    outs=[mfull_hbm.opt()],
                ).then_inc(sMB, 1)
        W_CONV = 16 * 2
        W_BIAS = 16 * 3
        W_WOUT = 16 * (3 + CCN)
        W_WIN = 16 * (3 + CCN + DCN)

        es0.close()
        es_p0.close()

        esA = contextlib.ExitStack()
        esA_p = contextlib.ExitStack()

        def sbA(name, shape, dtype):
            return esA.enter_context(nc.sbuf_tensor(name, shape, dtype))

        x_tm = [sbA(f"x_tm{s}", [128, TC, D], F32) for s in range(2)]
        x_bf = [sbA(f"x_bf{s}", [128, D], BF16) for s in range(2)]
        x_fm = [sbA(f"x_fm{s}", [128, DCN, T], BF16) for s in range(2)]
        sg = [sbA(f"sg{s}", [128, T], F32) for s in range(2)]
        glu = [sbA(f"glu{j}", [128, PAD + T], BF16) for j in range(NJ)]
        acc = [sbA(f"acc{s}", [128, T], F32) for s in range(2)]
        z1 = sbA("z1", [128, TC, D], F32)
        xh_tm = sbA("xh_tm", [128, TC, D], BF16)
        xl_tm = sbA("xl_tm", [128, TC, D], BF16)
        sqscr = sbA("sqscr", [128, D], BF16)
        sum1 = [sbA(f"sum1_{s}", [128, 4], F32) for s in range(2)]
        sumsq1 = [sbA(f"sumsq1_{s}", [128, 4], F32) for s in range(2)]
        stat1 = [sbA(f"stat1_{s}", [128, 16], F32) for s in range(2)]
        rb1 = [sbA(f"rb1_{s}", [128, 12], F32) for s in range(2)]
        pp = [esA_p.enter_context(nc.psum_tensor(f"pp{s}", [128, 1024], F32)) for s in range(4)]

        with nc.Block("pa") as blk:
            @blk.sync
            def _(sy):
                sy.wait_ge(sStA, 32)
                for i in range(NTILE):
                    if i >= 2:
                        sy.wait_ge(sZ1, 4 * (i - 1))
                    sy.dma_start(x_tm[i % 2][:], x_t[i]).then_inc(sX, 16)
                    for ts in range(TC):
                        tk = 4 * i + ts
                        sy.wait_ge(sXB, tk + 1)
                        if i >= 2 and ts == 0:
                            sy.wait_ge(sHa, NJ * (i - 1))
                        sy.dma_start(x_fm[i % 2][:, :, ts * 128:(ts + 1) * 128],
                                     x_bf[ts % 2][:], transpose=True).then_inc(sXF, 16)
                    if i >= 1:
                        sy.wait_ge(sXHc, 4 * i)
                        sy.dma_start(xh_t[i - 1], xh_tm[:]).then_inc(sSt, 16)
                        sy.wait_ge(sXL, 4 * i)
                        sy.dma_start(xl_t[i - 1], xl_tm[:]).then_inc(sSt, 16)
                sy.wait_ge(sXHc, 4 * NTILE)
                sy.dma_start(xh_t[NTILE - 1], xh_tm[:]).then_inc(sSt, 16)
                sy.wait_ge(sXL, 4 * NTILE)
                sy.dma_start(xl_t[NTILE - 1], xl_tm[:]).then_inc(sSt, 16)

            @blk.tensor
            def _(t):
                def _mm(*a, **k):
                    t.matmul(*a, **k)
                    if not _SIM:
                        t.nop(cycle_cnt=MMGAP)
                t.wait_ge(sWL, W_WIN)
                for i in range(NTILE):
                    for bh in range(NJ // 4):
                        js = [4 * bh + q for q in range(4)]
                        if bh == 0:
                            t.wait_ge(sXF, 16 * 4 * (i + 1))
                            if i >= 1:
                                t.wait_ge(sZ1, 4 * i)
                        for d in range(DCN):
                            for q in range(4):
                                j = js[q]
                                J = NJ * i + j
                                if d == 0 and J >= 4:
                                    t.wait_ge(sSG, J - 3)
                                _mm(pp[q][:, 0:512],
                                    w_in_bf[d][:, (NJ + j) * 128:(NJ + j + 1) * 128],
                                    x_fm[i % 2][:, d, :],
                                    start=(d == 0), stop=(d == DCN - 1))
                                if d == 0 and J >= 4:
                                    t.wait_ge(sGLU, J - 3)
                                _mm(pp[q][:, 512:1024],
                                    w_in_bf[d][:, j * 128:(j + 1) * 128],
                                    x_fm[i % 2][:, d, :],
                                    start=(d == 0), stop=(d == DCN - 1))
                        t.nop().then_inc(sHg, 4)
                        t.nop().then_inc(sHa, 4)
                    if i == 0:
                        t.wait_ge(sWL, W_WOUT)
                    t.wait_ge(sSI, NJ * (i + 1))
                    for c in range(CCN):
                        for ts in range(TC):
                            for db in range(2):
                                _mm(pp[ts][:, db * 512:(db + 1) * 512],
                                    glu[c][:, PAD + ts * 128:PAD + (ts + 1) * 128],
                                    w_out_bf[c][:, db * 512:(db + 1) * 512],
                                    start=(c == 0), stop=(c == CCN - 1))
                    t.nop().then_inc(sWO, 8)

            @blk.scalar
            def _(sc):
                for i in range(NTILE):
                    for ts in range(TC):
                        tk = 4 * i + ts
                        sc.wait_ge(sX, 16 * (i + 1))
                        if tk >= 2:
                            sc.wait_ge(sXF, 16 * (tk - 1))
                        sc.copy(x_bf[ts % 2][:], x_tm[i % 2][:, ts, :]).then_inc(sXB, 1)
                    if i >= 1:
                        for ts in range(TC):
                            z = 4 * (i - 1) + ts
                            sc.wait_ge(sX1, z + 1)
                            if i >= 2 and ts == 0:
                                sc.wait_ge(sSt, 32 * (i - 1))
                            sc.copy(xh_tm[:, ts, :], z1[:, ts, :]).then_inc(sXHc, 1)
                    for j in range(NJ):
                        J = NJ * i + j
                        sc.wait_ge(sHg, 4 * (4 * i + j // 4 + 1))
                        if J >= 2:
                            sc.wait_ge(sGLU, J - 1)
                        sc.activation(sg[j % 2][:], pp[j % 4][:, 0:512],
                                      AF.Sigmoid).then_inc(sSG, 1)
                        sc.wait_ge(sCV, J + 1)
                        sc.activation(glu[j][:, PAD:PAD + T], acc[j % 2][:],
                                      AF.Silu,
                                      bias=conv_b_sb[:, j:j + 1]).then_inc(sSI, 1)
                    for ts in range(TC):
                        z = 4 * i + ts
                        sc.wait_ge(sZ1, z + 1)
                        sc.activation(sqscr[:], z1[:, ts, :], AF.Square,
                                      accum_out=sumsq1[i % 2][:, ts:ts + 1]).then_inc(sSQ1, 1)
                    sc.wait_ge(sVE, i + 1)
                    sc.activation(rb1[i % 2][:, 8:12], stat1[i % 2][:, 8:12],
                                  AF.Sqrt).then_inc(sSD, 1)
                for ts in range(TC):
                    z = 4 * (NTILE - 1) + ts
                    sc.wait_ge(sX1, z + 1)
                    if ts == 0:
                        sc.wait_ge(sSt, 32 * (NTILE - 1))
                    sc.copy(xh_tm[:, ts, :], z1[:, ts, :]).then_inc(sXHc, 1)

            @blk.vector
            def _(v):
                vv_c = [0]
                v.wait_ge(sStA, 32)
                for i in range(NTILE):
                    if i % SEQ_TILES == 0:
                        if i > 0:
                            v.wait_ge(sWO, 8 * i)
                        for j in range(NJ):
                            v.memset(glu[j][:, 1:PAD], 0.0)
                    for j in range(NJ):
                        J = NJ * i + j
                        v.wait_ge(sSG, J + 1)
                        v.wait_ge(sHa, 4 * (4 * i + j // 4 + 1))
                        if j == 0 and i > 0 and i % SEQ_TILES != 0:
                            v.wait_ge(sWO, 8 * i)
                        if j == 0 and i == 0:
                            v.wait_ge(sWL, W_CONV)
                        v.tensor_mul(glu[j][:, PAD:PAD + T],
                                     pp[j % 4][:, 512:1024], sg[j % 2][:]).then_inc(sGLU, 1)
                        if J >= 2:
                            v.wait_ge(sSI, J - 1)
                        v.tensor_scalar_mul(acc[j % 2][:], glu[j][:, 1:1 + T],
                                            conv_w_sb[:, j, 0:1])
                        for k in range(1, 4):
                            v.scalar_tensor_tensor(acc[j % 2][:], glu[j][:, 1 + k:1 + k + T],
                                                   conv_w_sb[:, j, k:k + 1],
                                                   acc[j % 2][:], ALU.mult, ALU.add)
                        v.tensor_copy(glu[j][:, 1:PAD], glu[j][:, 1 + T:PAD + T]).then_inc(sCV, 1)
                    for ts in range(TC):
                        z = 4 * i + ts
                        v.wait_ge(sWO, 8 * (i + 1))
                        if ts == 0:
                            v.wait_ge(sX, 16 * (i + 1))
                        if z >= 4:
                            v.wait_ge(sSQ1, z - 3)
                            v.wait_ge(sXHc, z - 3)
                        v.scalar_tensor_tensor(z1[:, ts, :], pp[ts][:], 1.0,
                                               x_tm[i % 2][:, ts, :], ALU.mult, ALU.add,
                                               accum_out=sum1[i % 2][:, ts:ts + 1]).then_inc(sZ1, 1)
                    v.wait_ge(sSQ1, 4 * (i + 1))
                    v.wait_ge(sZ1, 4 * (i + 1))
                    st = stat1[i % 2]
                    v.tensor_scalar_mul(st[:, 0:4], sum1[i % 2][:], -1.0 / D)
                    v.tensor_scalar(st[:, 4:8], sumsq1[i % 2][:], 1.0 / D, EPS,
                                    ALU.mult, ALU.add)
                    v.tensor_scalar_mul(st[:, 8:12], sum1[i % 2][:],
                                        1.0 / D).then_inc(sVV, 1)
                    vv_c[0] += 1
                    v.wait_ge(sVV, vv_c[0])
                    v.tensor_mul(st[:, 12:16], st[:, 8:12], st[:, 0:4]).then_inc(sVV, 1)
                    vv_c[0] += 1
                    v.wait_ge(sVV, vv_c[0])
                    v.tensor_add(st[:, 8:12], st[:, 4:8], st[:, 12:16]).then_inc(sVE, 1)
                    v.wait_ge(sSD, i + 1)
                    v.reciprocal(rb1[i % 2][:, 0:4], rb1[i % 2][:, 8:12]).then_inc(sVV, 1)
                    vv_c[0] += 1
                    v.wait_ge(sVV, vv_c[0])
                    v.tensor_mul(rb1[i % 2][:, 4:8], st[:, 0:4],
                                 rb1[i % 2][:, 0:4]).then_inc(sVV, 1)
                    vv_c[0] += 1
                    v.wait_ge(sVV, vv_c[0])
                    for ts in range(TC):
                        z = 4 * i + ts
                        v.tensor_scalar_mul(z1[:, ts, :], z1[:, ts, :],
                                            rb1[i % 2][:, ts:ts + 1])
                        v.tensor_scalar_add(z1[:, ts, :], z1[:, ts, :],
                                            rb1[i % 2][:, 4 + ts:5 + ts]).then_inc(sX1, 1)
                    for ts in range(TC):
                        z = 4 * i + ts
                        v.wait_ge(sXHc, z + 1)
                        if ts == 0 and i >= 1:
                            v.wait_ge(sSt, 32 * i)
                        v.tensor_sub(xl_tm[:, ts, :], z1[:, ts, :],
                                     xh_tm[:, ts, :]).then_inc(sXL, 1)

        esA.close()
        esA_p.close()
        es_wb.close()

        esB = contextlib.ExitStack()
        esB_p = contextlib.ExitStack()

        def sbB(name, shape, dtype):
            return esB.enter_context(nc.sbuf_tensor(name, shape, dtype))

        Mh2 = sbB("Mh2", [128, DCN, D], BF16)
        Ml2 = sbB("Ml2", [128, DCN, D], BF16)
        xh2 = [sbB(f"xh2_{s}", [128, TC, D], BF16) for s in range(2)]
        xl2 = [sbB(f"xl2_{s}", [128, TC, D], BF16) for s in range(2)]
        xfh = [sbB(f"xfh{s}", [128, DCN, T], BF16) for s in range(2)]
        xfl = [sbB(f"xfl{s}", [128, DCN, T], BF16) for s in range(2)]
        z2 = sbB("z2", [128, TC, D], F32)
        outb = [sbB(f"outb{s}", [128, TC, D], F32) for s in range(2)]
        sqscr2 = sbB("sqscr2", [128, D], BF16)
        sum2 = [sbB(f"sum2_{s}", [128, 4], F32) for s in range(2)]
        sumsq2 = [sbB(f"sumsq2_{s}", [128, 4], F32) for s in range(2)]
        stat2 = [sbB(f"stat2_{s}", [128, 16], F32) for s in range(2)]
        rb2 = [sbB(f"rb2_{s}", [128, 12], F32) for s in range(2)]
        pf = [esB_p.enter_context(nc.psum_tensor(f"pf{s}", [128, D], F32)) for s in range(4)]

        with nc.Block("pb") as blk:
            @blk.sync
            def _(sy):
                sy.wait_ge(sSt, 32 * NTILE)
                sy.wait_ge(sMB, 1)
                sy.dma_start(Mh2[:], mfull_h[0]).then_inc(sMB, 16)
                sy.dma_start(Ml2[:], mfull_h[1]).then_inc(sMB, 16)
                for i in range(NTILE):
                    if i >= 2:
                        sy.wait_ge(sZ2, 4 * (i - 1))
                    sy.dma_start(xh2[i % 2][:], xh_t[i]).then_inc(sBX, 16)
                    sy.dma_start(xl2[i % 2][:], xl_t[i]).then_inc(sBX, 16)
                    if i >= 1:
                        sy.wait_ge(sOut, 4 * i)
                        sy.dma_start(out_t[i - 1], outb[(i - 1) % 2][:]).then_inc(sOS, 16)
                sy.wait_ge(sOut, 4 * NTILE)
                sy.dma_start(out_t[NTILE - 1], outb[(NTILE - 1) % 2][:]).then_inc(sOS, 16)
                sy.wait_ge(sOS, 16 * NTILE)

            @blk.scalar
            def _(sc):
                for i in range(NTILE):
                    for ts in range(TC):
                        sc.wait_ge(sBX, 32 * (i + 1))
                        if i >= 2 and ts == 0:
                            sc.wait_ge(sFB, 8 * (i - 1))
                        sc.dma_start(xfh[i % 2][:, :, ts * 128:(ts + 1) * 128],
                                     xh2[i % 2][:, ts, :], transpose=True).then_inc(sBF, 16)
                        sc.dma_start(xfl[i % 2][:, :, ts * 128:(ts + 1) * 128],
                                     xl2[i % 2][:, ts, :], transpose=True).then_inc(sBF, 16)
                    for ts in range(TC):
                        z = 4 * i + ts
                        sc.wait_ge(sZ2, z + 1)
                        sc.activation(sqscr2[:], z2[:, ts, :], AF.Square,
                                      accum_out=sumsq2[i % 2][:, ts:ts + 1]).then_inc(sSQ2, 1)
                    sc.wait_ge(sVE2, i + 1)
                    sc.activation(rb2[i % 2][:, 8:12], stat2[i % 2][:, 8:12],
                                  AF.Sqrt).then_inc(sSD2, 1)

            @blk.tensor
            def _(t):
                def _mm(*a, **k):
                    t.matmul(*a, **k)
                    if not _SIM:
                        t.nop(cycle_cnt=MMGAP)
                t.wait_ge(sWL, W_BIAS)
                t.wait_ge(sMB, 33)
                for i in range(NTILE):
                    t.wait_ge(sBF, 16 * 8 * (i + 1))
                    if i >= 1:
                        t.wait_ge(sZ2, 4 * i)
                    for step in range(25):
                        for ts in range(TC):
                            for ob in range(2):
                                psl = pf[ts][:, ob * 512:(ob + 1) * 512]
                                if step == 0:
                                    _mm(psl, ones_bf[:],
                                        bias_bf[:, ob * 512:(ob + 1) * 512],
                                        start=True, stop=False)
                                else:
                                    s = step - 1
                                    ic, which = s // 3, s % 3
                                    lx = xfh[i % 2][:, ic, ts * 128:(ts + 1) * 128]
                                    ll = xfl[i % 2][:, ic, ts * 128:(ts + 1) * 128]
                                    if which == 0:
                                        _mm(psl, lx, Mh2[:, ic, ob * 512:(ob + 1) * 512],
                                            start=False, stop=False)
                                    elif which == 1:
                                        _mm(psl, lx, Ml2[:, ic, ob * 512:(ob + 1) * 512],
                                            start=False, stop=False)
                                    else:
                                        _mm(psl, ll, Mh2[:, ic, ob * 512:(ob + 1) * 512],
                                            start=False, stop=(step == 24))
                    t.nop().then_inc(sFB, 8)

            @blk.vector
            def _(v):
                vv_c = [32]
                for i in range(NTILE):
                    for ts in range(TC):
                        z = 4 * i + ts
                        v.wait_ge(sFB, 8 * (i + 1))
                        if z >= 4:
                            v.wait_ge(sSQ2, z - 3)
                        v.tensor_add(z2[:, ts, :], pf[ts][:], xh2[i % 2][:, ts, :])
                        v.scalar_tensor_tensor(z2[:, ts, :], xl2[i % 2][:, ts, :], 1.0,
                                               z2[:, ts, :], ALU.mult, ALU.add,
                                               accum_out=sum2[i % 2][:, ts:ts + 1]).then_inc(sZ2, 1)
                    v.wait_ge(sSQ2, 4 * (i + 1))
                    v.wait_ge(sZ2, 4 * (i + 1))
                    st = stat2[i % 2]
                    v.tensor_scalar_mul(st[:, 0:4], sum2[i % 2][:], -1.0 / D)
                    v.tensor_scalar(st[:, 4:8], sumsq2[i % 2][:], 1.0 / D, EPS,
                                    ALU.mult, ALU.add)
                    v.tensor_scalar_mul(st[:, 8:12], sum2[i % 2][:],
                                        1.0 / D).then_inc(sVV, 1)
                    vv_c[0] += 1
                    v.wait_ge(sVV, vv_c[0])
                    v.tensor_mul(st[:, 12:16], st[:, 8:12], st[:, 0:4]).then_inc(sVV, 1)
                    vv_c[0] += 1
                    v.wait_ge(sVV, vv_c[0])
                    v.tensor_add(st[:, 8:12], st[:, 4:8], st[:, 12:16]).then_inc(sVE2, 1)
                    v.wait_ge(sSD2, i + 1)
                    v.reciprocal(rb2[i % 2][:, 0:4], rb2[i % 2][:, 8:12]).then_inc(sVV, 1)
                    vv_c[0] += 1
                    v.wait_ge(sVV, vv_c[0])
                    v.tensor_mul(rb2[i % 2][:, 4:8], st[:, 0:4],
                                 rb2[i % 2][:, 0:4]).then_inc(sVV, 1)
                    vv_c[0] += 1
                    v.wait_ge(sVV, vv_c[0])
                    for ts in range(TC):
                        z = 4 * i + ts
                        if i >= 2 and ts == 0:
                            v.wait_ge(sOS, 16 * (i - 1))
                        v.tensor_scalar_mul(outb[i % 2][:, ts, :], z2[:, ts, :],
                                            rb2[i % 2][:, ts:ts + 1])
                        v.tensor_scalar_add(outb[i % 2][:, ts, :], outb[i % 2][:, ts, :],
                                            rb2[i % 2][:, 4 + ts:5 + ts]).then_inc(sOut, 1)

            @blk.gpsimd
            def _(gp):
                gp.engine_nop()

        esB.close()
        esB_p.close()
    return nc


def _host_prepare(inputs):
    x = np.ascontiguousarray(np.asarray(inputs["L0_attn_x"], dtype=np.float32))
    w_in = np.ascontiguousarray(np.asarray(inputs["L0_attn_W_in"], dtype=np.float32))
    conv_w = np.asarray(inputs["L0_attn_conv_w"], dtype=np.float32)
    conv_b = np.asarray(inputs["L0_attn_conv_b"], dtype=np.float32)
    w_out = np.ascontiguousarray(np.asarray(inputs["L0_attn_W_out"], dtype=np.float32))
    wr = np.ascontiguousarray(np.asarray(inputs["L0_ffn_w_freq_real"], dtype=np.float32))
    wi = np.ascontiguousarray(np.asarray(inputs["L0_ffn_w_freq_imag"], dtype=np.float32))
    bias = np.ascontiguousarray(np.asarray(inputs["L0_ffn_bias"], dtype=np.float32).reshape(1, D))

    conv_w_r = np.ascontiguousarray(conv_w.reshape(CCN, 128, 4).transpose(1, 0, 2))
    conv_b_r = np.ascontiguousarray(conv_b.reshape(CCN, 128).T)

    n = np.arange(D, dtype=np.float64)
    ang = (2.0 * np.pi / D) * np.outer(n, n)
    c64 = np.cos(ang)
    s64 = np.sin(ang)
    ca = c64.astype(np.float32)
    cb = (c64 - ca.astype(np.float64)).astype(np.float32)
    sa = s64.astype(np.float32)
    sb_ = (s64 - sa.astype(np.float64)).astype(np.float32)

    common = {
        "w_in": w_in, "conv_w": conv_w_r, "conv_b": conv_b_r, "w_out": w_out,
        "wr": wr, "wi": wi, "ca": np.ascontiguousarray(ca),
        "cb": np.ascontiguousarray(cb), "sa": np.ascontiguousarray(sa),
        "sb": np.ascontiguousarray(sb_), "bias": bias,
    }
    B = x.shape[0]
    per = B // N_CORES
    in_maps = []
    for c in range(N_CORES):
        m = dict(common)
        m["x"] = np.ascontiguousarray(x[c * per:(c + 1) * per].reshape(NTOK, D))
        sl = slice(128 * c, 128 * (c + 1))
        m["cas"] = np.ascontiguousarray(ca[:, sl])
        m["cbs"] = np.ascontiguousarray(cb[:, sl])
        m["sas"] = np.ascontiguousarray(sa[:, sl])
        m["sbs"] = np.ascontiguousarray(sb_[:, sl])
        in_maps.append(m)
    return in_maps, x.shape


def kernel(L0_attn_x, L0_attn_W_in, L0_attn_conv_w, L0_attn_conv_b,
           L0_attn_W_out, L0_ffn_w_freq_real, L0_ffn_w_freq_imag, L0_ffn_bias):
    inputs = {
        "L0_attn_x": L0_attn_x, "L0_attn_W_in": L0_attn_W_in,
        "L0_attn_conv_w": L0_attn_conv_w, "L0_attn_conv_b": L0_attn_conv_b,
        "L0_attn_W_out": L0_attn_W_out, "L0_ffn_w_freq_real": L0_ffn_w_freq_real,
        "L0_ffn_w_freq_imag": L0_ffn_w_freq_imag, "L0_ffn_bias": L0_ffn_bias,
    }
    in_maps, xshape = _host_prepare(inputs)
    nc = build_program()
    res = run_bass_kernel_spmd(nc, in_maps, core_ids=list(range(N_CORES)))
    per = xshape[0] // N_CORES
    out = np.empty(xshape, dtype=np.float32)
    for c in range(N_CORES):
        out[c * per:(c + 1) * per] = np.asarray(
            res.results[c]["out"]).reshape(per, xshape[1], xshape[2])
    return out
